# version 01
# baseline (speedup 1.0000x reference)
import numpy as np
import concourse.bass as bass
import concourse.mybir as mybir
from concourse.bass_utils import run_bass_kernel_spmd
from concourse.bass_core_rust import IndirectOffsetOnAxis
from contextlib import ExitStack

F32 = mybir.dt.float32
BF16 = mybir.dt.bfloat16
I32 = mybir.dt.int32
U32 = mybir.dt.uint32
AF = mybir.ActivationFunctionType
ALU = mybir.AluOpType
AX = mybir.AxisListType

NCORES = 8
T = 2048
C16 = 16
D = 1024
DF = 2816
FC = 22
NFB = 11
NEXP = 9
NEG = -3.0e38
POLLS = 400

BUILD_STAGE = 5


def build_nc(stage=None):
    stage = BUILD_STAGE if stage is None else stage
    nc = bass.Bass("TRN2", target_bir_lowering=False, debug=False)

    def din(name, shape, dt=F32):
        return nc.dram_tensor(name, shape, dt, kind="ExternalInput").ap()

    x_in = din("x", [T, D])
    out = nc.dram_tensor("out", [T, D], F32, kind="ExternalOutput").ap()
    dbg_cw = nc.dram_tensor("dbg_cw", [8, T], F32, kind="ExternalOutput").ap()
    dbg_xh = nc.dram_tensor("dbg_xh", [128, 8, 128], BF16, kind="ExternalOutput").ap()
    dbg_lg = nc.dram_tensor("dbg_lg", [128, C16, 8], F32, kind="ExternalOutput").ap()
    dbg_g = nc.dram_tensor("dbg_g", [128, 24], F32, kind="ExternalOutput").ap()
    dbg_sel = nc.dram_tensor("dbg_sel", [128, 192], F32, kind="ExternalOutput").ap()
    dbg_idx = nc.dram_tensor("dbg_idx", [128, 2, 8], U32, kind="ExternalOutput").ap()
    dbg_idx2 = nc.dram_tensor("dbg_idx2", [128, 2, 8], U32, kind="ExternalOutput").ap()
    dbg_s = nc.dram_tensor("dbg_s", [128, 64], F32, kind="ExternalOutput").ap()
    dbg_lq = nc.dram_tensor("dbg_lq", [128, 4, 2048], BF16, kind="ExternalOutput").ap()
    dbg_lk = nc.dram_tensor("dbg_lk", [128, 4, 2304], BF16, kind="ExternalOutput").ap()
    dbg_loh = nc.dram_tensor("dbg_loh", [128, 4, 2048], BF16, kind="ExternalOutput").ap()
    dbg_lw = nc.dram_tensor("dbg_lw", [128, 4, 1024], BF16, kind="ExternalOutput").ap()

    w1t = din("w1t", [NEXP, D, DF])
    w2t = din("w2t", [NEXP, DF, D])
    w3t = din("w3t", [NEXP, D, DF])
    gwt = din("gwt", [D, 8])

    ln_g = din("ln_g", [1, D])
    ln_b = din("ln_b", [1, D])
    qkv_w = din("qkv_w", [D, 1536])
    lout_w = din("lout_w", [512, D])
    rtoks = din("rtoks", [1, 2 * D])
    hgam = din("hgam", [1, D])
    hq_w = din("hq_w", [D, 512])
    hkv_wk = din("hkv_wk", [D, 512])
    hkv_wv = din("hkv_wv", [D, 512])
    hnull_c = din("hnull_c", [64, 16])
    hnull_vrow = din("hnull_vrow", [1, 8, 64])
    hout_w = din("hout_w", [512, D])
    nullq = din("nullq", [1, D])

    cos_t = din("cos_t", [4096, 64])
    sin_t = din("sin_t", [4096, 64])
    onesrow_c = din("onesrow", [1, 128])
    onescol_c = din("onescol", [128, 1])
    iota8_c = din("iota8", [128, 8])
    iotar_c = din("iotar", [128, 1024])
    iotaP_c = din("iotaP", [128, 32, 4], BF16)
    triI_c = din("triI", [128, 128])
    headE_c = din("headE", [8, 4, 128], BF16)
    masks_c = din("masks", [4, 6, 128, 512], BF16)
    sel0_c = din("sel0", [128, 16])
    sel1_c = din("sel1", [128, 16])
    zw_idx_c = din("zw_idx", [128, 16], U32)
    halo_idx_c = din("halo_idx", [128, 2], U32)
    magic_c = din("magicrow", [1, 16], I32)
    fwr_idx_c = din("fwr_idx", [2], U32)
    poll_idx_c = din("poll_idx", [2, 1], U32)
    srow_idx_c = din("srow_idx", [2], U32)

    z_pair = nc.dram_tensor("z_pair", [4096, D], F32, addr_space="Shared")
    s_pair = nc.dram_tensor("s_pair", [4, 2048], F32, addr_space="Shared")
    f_pair = nc.dram_tensor("f_pair", [16, 16], I32, addr_space="Shared")
    heavy_d = nc.dram_tensor("heavy_d", [1025, D], F32)
    sc_d = nc.dram_tensor("sc_d", [4096, 16], F32)
    cw_d = nc.dram_tensor("cw_d", [8, T], F32)

    class Ctr:
        __slots__ = ("h", "n")

        def __init__(self, h):
            self.h = h
            self.n = 0

        def inc(self, amt=1):
            self.n += amt
            return self.n

    top = ExitStack()
    with top:
        def sb(name, shape, dt=F32, stack=None):
            return (stack or top).enter_context(nc.sbuf_tensor(name, shape, dt))

        def mkpsum(name, shape, dt=F32, stack=None):
            return (stack or top).enter_context(nc.psum_tensor(name, shape, dt))

        RES = sb("RES", [128, C16, D])
        CWb = sb("CWb", [128, T], BF16)
        consts = sb("consts", [128, 8])
        onesrow = sb("onesrow_s", [1, 128])
        onescol = sb("onescol_s", [128, 1])
        onesrowb = sb("onesrowb_s", [1, 128], BF16)
        onescolb = sb("onescolb_s", [128, 1], BF16)
        iota8 = sb("iota8_s", [128, 8])
        triI = sb("triI_s", [128, 128])
        identF = sb("identF", [128, 128])
        identB = sb("identB", [128, 128], BF16)
        iint = sb("iint", [128, 128], I32)
        gwb = sb("gwb", [128, 8, 8], BF16)
        dbg_lgt_keep = sb("dbg_lgt_keep", [128, C16, 8])
        dbg_g_keep = sb("dbg_g_keep", [128, 24])
        cwTs8 = sb("cwTs8", [8, 128])

        sW = Ctr(top.enter_context(nc.semaphore("sW")))
        sD = Ctr(top.enter_context(nc.semaphore("sD")))
        sSc = Ctr(top.enter_context(nc.semaphore("sSc")))
        sT = Ctr(top.enter_context(nc.semaphore("sT")))
        sV = Ctr(top.enter_context(nc.semaphore("sV")))
        sA = Ctr(top.enter_context(nc.semaphore("sA")))
        sG = Ctr(top.enter_context(nc.semaphore("sG")))
        sB = Ctr(top.enter_context(nc.semaphore("sB")))
        sO = Ctr(top.enter_context(nc.semaphore("sO")))

        block = top.enter_context(nc.Block())

        @block.sync
        def _(sync):
            for c in range(C16):
                sync.dma_start(out=RES[:, c, :], in_=x_in[c * 128:(c + 1) * 128, :]).then_inc(sD.h, 16); sD.inc(16)
            sync.dma_start(out=onesrow[:, :], in_=onesrow_c[:, :]).then_inc(sD.h, 16); sD.inc(16)
            sync.dma_start(out=onescol[:, :], in_=onescol_c[:, :]).then_inc(sD.h, 16); sD.inc(16)
            sync.dma_start(out=iota8[:, :], in_=iota8_c[:, :]).then_inc(sD.h, 16); sD.inc(16)
            sync.dma_start(out=triI[:, :], in_=triI_c[:, :]).then_inc(sD.h, 16); sD.inc(16)
        sD_pre = sD.n

        @block.gpsimd
        def _(g):
            g.dma_start(out=onesrowb[:, :], in_=onesrow_c[:, :]).then_inc(sW.h, 16); sW.inc(16)
            g.dma_start(out=onescolb[:, :], in_=onescol_c[:, :]).then_inc(sW.h, 16); sW.inc(16)
            for d in range(8):
                g.dma_start(out=gwb[:, d, :], in_=gwt[d * 128:(d + 1) * 128, :]).then_inc(sW.h, 16); sW.inc(16)
            g.memset(consts[:, :], 0.0)
            g.memset(consts[:, 1:2], 1e-5)
            g.memset(consts[:, 2:3], 1e-6)
            g.iota(iint[:, :], pattern=[[-1, 128]], base=0, channel_multiplier=1).then_inc(sG.h, 1); sG.inc()
        sW_pre = sW.n
        sG_pre = sG.n

        @block.vector
        def _(v):
            v.wait_ge(sG.h, sG_pre)
            v.tensor_scalar(identB[:, :], iint[:, :], 0.0, None, ALU.is_equal)
            v.tensor_scalar(identF[:, :], iint[:, :], 0.0, None, ALU.is_equal).then_inc(sV.h, 1); sV.inc()
        sV_pre = sV.n

        def emit_ffn(tag, first, prior_done_T, prior_done_V=0):
            with ExitStack() as st:
                XH = sb(f"XH{tag}", [128, 8, C16, 128], BF16, st)
                w1s = sb(f"w1s{tag}", [128, 2, 8, 256], BF16, st)
                w3s = sb(f"w3s{tag}", [128, 2, 8, 256], BF16, st)
                w2s = sb(f"w2s{tag}", [128, 2, FC, 256], BF16, st)
                gel = sb(f"gel{tag}", [128, 2, 512], BF16, st)
                h3s = sb(f"h3s{tag}", [128, 2, 512], BF16, st)
                ps_h1 = [mkpsum(f"ph1{tag}{i}", [128, 512], F32, st) for i in range(2)]
                ps_h3 = [mkpsum(f"ph3{tag}{i}", [128, 512], F32, st) for i in range(2)]
                ps_y = [mkpsum(f"py{tag}{i}", [128, 256], F32, st) for i in range(2)]
                with ExitStack() as lst:
                    XLO = sb(f"XLO{tag}", [128, 8, C16, 128], BF16, lst)
                    xn_tok = sb(f"xnt{tag}", [128, 2, D], BF16, lst)
                    xl_tok = sb(f"xlt{tag}", [128, 2, D], BF16, lst)
                    y1b = sb(f"y1{tag}", [128, 2, D], F32, lst)
                    stat = sb(f"stat{tag}", [128, 16], F32, lst)
                    lgt = sb(f"lgt{tag}", [128, C16, 8], F32, lst)
                    gsc = sb(f"gsc{tag}", [128, 12], F32, lst)
                    gidx = sb(f"gidx{tag}", [128, 8], U32, lst)
                    gidxf = sb(f"gidxf{tag}", [128, 2], F32, lst)
                    cwc = sb(f"cwc{tag}", [128, 8], F32, lst)
                    cw2 = sb(f"cw2{tag}", [128, 8], F32, lst)
                    sqt = sb(f"sqt{tag}", [128, 1024], F32, lst)
                    ps_g = [ps_h1[0][:, 0:8], ps_h1[1][:, 0:8]]
                    ps_t8 = ps_y[0][0:8, 0:128]

                    V0, A0, S0 = sV.n, sA.n, sSc.n

                    @block.vector
                    def _(v):
                        v.wait_ge(sT.h, prior_done_T)
                        for c in range(C16):
                            r = c % 2
                            so = r * 8
                            if first:
                                v.wait_ge(sD.h, 16 * (c + 1))
                            if c >= 2:
                                v.wait_ge(sA.h, A0 + 2 * (c - 1))
                                v.wait_ge(sSc.h, S0 + 32 * (c - 1))
                            v.tensor_reduce(stat[:, so:so + 1], RES[:, c, :], axis=AX.X, op=ALU.add)
                            v.drain()
                            v.tensor_scalar(stat[:, so + 1:so + 2], stat[:, so:so + 1], -1.0 / D,
                                            None, ALU.mult).then_inc(sV.h, 1); sV.inc()
                            v.wait_ge(sA.h, A0 + 2 * c + 1)
                            v.drain()
                            v.tensor_scalar(y1b[:, r, :], RES[:, c, :], stat[:, so + 1:so + 2],
                                            stat[:, so + 3:so + 4], ALU.add, ALU.mult)
                            v.scalar_tensor_tensor(RES[:, c, :], y1b[:, r, :], 0.5, RES[:, c, :],
                                                   ALU.mult, ALU.add).then_inc(sV.h, 1); sV.inc()
                            v.wait_ge(sA.h, A0 + 2 * c + 2)
                            v.drain()
                            v.tensor_scalar(xn_tok[:, r, :], y1b[:, r, :], stat[:, so + 5:so + 6],
                                            None, ALU.mult)
                            v.tensor_scalar(y1b[:, 1 - r, :], y1b[:, r, :], stat[:, so + 5:so + 6],
                                            None, ALU.mult)
                            v.tensor_sub(xl_tok[:, r, :], y1b[:, 1 - r, :], xn_tok[:, r, :]).then_inc(sV.h, 1); sV.inc()
                    assert sV.n == V0 + 3 * C16

                    @block.scalar
                    def _(s):
                        s.wait_ge(sG.h, sG_pre)
                        s.wait_ge(sT.h, prior_done_T)
                        for c in range(C16):
                            r = c % 2
                            so = r * 8
                            s.wait_ge(sV.h, V0 + 3 * c + 1)
                            s.activation(sqt[:, :], RES[:, c, :], AF.Square,
                                         bias=stat[:, so + 1:so + 2], accum_out=stat[:, so + 2:so + 3])
                            s.drain()
                            s.mul(stat[:, so + 6:so + 7], stat[:, so + 2:so + 3], 1.0 / D)
                            s.drain()
                            s.activation(stat[:, so + 7:so + 8], stat[:, so + 6:so + 7], AF.Ln,
                                         bias=consts[:, 1:2])
                            s.drain()
                            s.activation(stat[:, so + 3:so + 4], stat[:, so + 7:so + 8], AF.Exp,
                                         scale=-0.5).then_inc(sA.h, 1); sA.inc()
                            s.wait_ge(sV.h, V0 + 3 * c + 2)
                            s.activation(sqt[:, :], y1b[:, r, :], AF.Square,
                                         bias=consts[:, 0:1], accum_out=stat[:, so + 4:so + 5])
                            s.drain()
                            s.mul(stat[:, so + 6:so + 7], stat[:, so + 4:so + 5], 1.0 / D)
                            s.drain()
                            s.activation(stat[:, so + 7:so + 8], stat[:, so + 6:so + 7], AF.Ln,
                                         bias=consts[:, 2:3])
                            s.drain()
                            s.activation(stat[:, so + 5:so + 6], stat[:, so + 7:so + 8], AF.Exp,
                                         scale=-0.5).then_inc(sA.h, 1); sA.inc()
                            s.wait_ge(sV.h, V0 + 3 * c + 3)
                            s.dma_start_transpose(XH[:, :, c, :], xn_tok[:, r, :]).then_inc(sSc.h, 16); sSc.inc(16)
                            s.dma_start_transpose(XLO[:, :, c, :], xl_tok[:, r, :]).then_inc(sSc.h, 16); sSc.inc(16)
                    assert sA.n == A0 + 2 * C16 and sSc.n == S0 + 32 * C16
                    S_tr_end = sSc.n

                    T1, V1 = sT.n, sV.n

                    @block.tensor
                    def _(t):
                        t.wait_ge(sSc.h, S_tr_end)
                        t.wait_ge(sW.h, sW_pre)
                        for c in range(C16):
                            if c >= 2:
                                t.wait_ge(sV.h, V1 + (c - 1))
                            pg = ps_g[c % 2]
                            for m in range(8):
                                t.matmul(pg, XH[:, m, c, :], gwb[:, m, :], start=(m == 0), stop=False)
                            for m in range(8):
                                mm = t.matmul(pg, XLO[:, m, c, :], gwb[:, m, :],
                                              start=False, stop=(m == 7))
                            mm.then_inc(sT.h, 1); sT.inc()
                    assert sT.n == T1 + C16

                    @block.vector
                    def _(v):
                        for c in range(C16):
                            v.wait_ge(sT.h, T1 + c + 1)
                            v.tensor_copy(lgt[:, c, :], ps_g[c % 2])
                            v.tensor_copy(dbg_lgt_keep[:, c, :], ps_g[c % 2]).then_inc(sV.h, 1); sV.inc()
                    assert sV.n == V1 + C16

                    V2, A2, T2 = sV.n, sA.n, sT.n
                    cwT_W0 = sW.n

                    @block.vector
                    def _(v):
                        v.wait_ge(sD.h, sD_pre)
                        v.drain()
                        for c in range(C16):
                            v.max(out=gsc[:, 0:8], in_=lgt[:, c, :])
                            v.drain()
                            v.tensor_sub(gsc[:, 8:9], gsc[:, 1:2], gsc[:, 0:1]).then_inc(sV.h, 1); sV.inc()
                            v.wait_ge(sA.h, A2 + c + 1)
                            v.drain()
                            v.tensor_scalar(gsc[:, 10:11], gsc[:, 9:10], 1.0, None, ALU.add)
                            v.drain()
                            v.reciprocal(gsc[:, 10:11], gsc[:, 10:11])
                            v.drain()
                            v.tensor_scalar(gsc[:, 11:12], gsc[:, 10:11], -1.0, 1.0, ALU.mult, ALU.add).then_inc(sV.h, 1); sV.inc()
                            v.wait_ge(sV.h, V2 + 4 * c + 2)
                            v.drain()
                            v.tensor_scalar(cwc[:, 0:8], lgt[:, c, :], gsc[:, 0:1], gsc[:, 10:11],
                                            ALU.is_equal, ALU.mult)
                            v.tensor_scalar(cw2[:, 0:8], lgt[:, c, :], gsc[:, 1:2], gsc[:, 11:12],
                                            ALU.is_equal, ALU.mult)
                            v.drain()
                            v.tensor_add(cwc[:, 0:8], cwc[:, 0:8], cw2[:, 0:8]).then_inc(sV.h, 1); sV.inc()
                            if c == 0 and tag == "f1":
                                v.tensor_copy(dbg_g_keep[:, 0:12], gsc[:, 0:12])
                                v.tensor_copy(dbg_g_keep[:, 12:20], cwc[:, 0:8])
                            v.wait_ge(sT.h, T2 + c + 1)
                            if c >= 1:
                                v.wait_ge(sW.h, cwT_W0 + 16 * c)
                            v.tensor_copy(cwTs8[:, :], ps_t8).then_inc(sV.h, 1); sV.inc()
                    assert sV.n == V2 + 4 * C16

                    @block.scalar
                    def _(s):
                        for c in range(C16):
                            s.wait_ge(sV.h, V2 + 4 * c + 1)
                            s.activation(gsc[:, 9:10], gsc[:, 8:9], AF.Exp).then_inc(sA.h, 1); sA.inc()
                    assert sA.n == A2 + C16

                    @block.gpsimd
                    def _(g):
                        for c in range(C16):
                            g.wait_ge(sV.h, V2 + 4 * c + 4)
                            g.dma_start(out=cw_d[:, c * 128:(c + 1) * 128], in_=cwTs8[:, :]).then_inc(sW.h, 16); sW.inc(16)

                    @block.tensor
                    def _(t):
                        t.wait_ge(sV.h, sV_pre)
                        for c in range(C16):
                            t.wait_ge(sV.h, V2 + 4 * c + 3)
                            if c >= 1:
                                t.wait_ge(sV.h, V2 + 4 * (c - 1) + 4)
                            t.transpose(ps_t8, cwc[:, 0:8], identF[:, :]).then_inc(sT.h, 1); sT.inc()
                    assert sT.n == T2 + C16

                gate_done_V = sV.n
                gate_done_T = sT.n
                gate_done_W = sW.n

                H = sb(f"H{tag}", [128, FC, 1024], BF16, st)

                Tbase, Vbase = sT.n, sV.n
                tv, vv = Tbase, Vbase
                t_fb_done = {}
                t_y_done = {}
                v_ph1_done = {}
                for e in range(NEXP):
                    for h in range(2):
                        for fb in range(NFB):
                            tv += 8
                            t_fb_done[(e, h, fb)] = tv
                        vv += 2 * 2 * FC
                        v_ph1_done[(e, h)] = vv
                        for mq in range(4):
                            tv += 8
                            vv += 8
                            t_y_done[(e, h, mq)] = tv
                t_virtual, v_virtual = tv, vv

                w1_tgt = {}
                w2_tgt = {}
                slab_seq = []
                w2_seq = []
                for e in range(NEXP):
                    for h in range(2):
                        for fb in range(NFB):
                            slab_seq.append((e, h, fb))
                        for mq in range(4):
                            w2_seq.append((e, h, mq))

                cwb_tgt = {}

                @block.gpsimd
                def _(g):
                    g.wait_ge(sT.h, prior_done_T)
                    g.wait_ge(sV.h, prior_done_V)
                    i1 = i2 = 0
                    for e in range(NEXP):
                        if e < 8:
                            g.wait_ge(sW.h, gate_done_W)
                            if e >= 1:
                                g.wait_ge(sV.h, v_ph1_done[(e - 1, 1)])
                            g.dma_start(out=CWb[:, :],
                                        in_=cw_d[e:e + 1, :].broadcast_to((128, T))
                                        ).then_inc(sW.h, 16); sW.inc(16)
                        cwb_tgt[e] = sW.n
                        for h in range(2):
                            for fb in range(NFB):
                                if i1 >= 2:
                                    g.wait_ge(sT.h, t_fb_done[slab_seq[i1 - 2]])
                                f0 = fb * 256
                                src1 = w1t[e].rearrange("(m p) f -> p m f", p=128)
                                src3 = w3t[e].rearrange("(m p) f -> p m f", p=128)
                                g.dma_start(out=w1s[:, fb % 2, :, :],
                                            in_=src1[:, :, f0:f0 + 256]).then_inc(sW.h, 16); sW.inc(16)
                                g.dma_start(out=w3s[:, fb % 2, :, :],
                                            in_=src3[:, :, f0:f0 + 256]).then_inc(sW.h, 16); sW.inc(16)
                                w1_tgt[(e, h, fb)] = sW.n
                                i1 += 1
                            for mq in range(4):
                                if i2 >= 2:
                                    g.wait_ge(sT.h, t_y_done[w2_seq[i2 - 2]])
                                src2 = w2t[e].rearrange("(f p) m -> p f m", p=128)
                                g.dma_start(out=w2s[:, mq % 2, :, :],
                                            in_=src2[:, :, mq * 256:(mq + 1) * 256]).then_inc(sW.h, 16); sW.inc(16)
                                w2_tgt[(e, h, mq)] = sW.n
                                i2 += 1

                for e in range(NEXP):
                    for h in range(2):
                        iters = [(f, u) for f in range(FC) for u in range(2)]
                        T3, V3, A3 = sT.n, sV.n, sA.n

                        @block.tensor
                        def _(t, e=e, h=h, T3=T3, V3=V3, A3=A3):
                            if e == 0 and h == 0:
                                t.wait_ge(sV.h, gate_done_V)
                            if e < 8 and h == 0:
                                t.wait_ge(sW.h, cwb_tgt[e])
                            for i, (f, u) in enumerate(iters):
                                fb, fi = f // 2, f % 2
                                if fi == 0 and u == 0:
                                    t.wait_ge(sW.h, w1_tgt[(e, h, fb)])
                                if i >= 2:
                                    t.wait_ge(sA.h, A3 + (i - 1))
                                    t.wait_ge(sV.h, V3 + 2 * (i - 2) + 1)
                                b = i % 2
                                for d in range(8):
                                    mm = t.matmul(ps_h1[b][:, :],
                                                  w1s[:, fb % 2, d, fi * 128:fi * 128 + 128],
                                                  XH[:, d, 8 * h + 4 * u:8 * h + 4 * u + 4, :],
                                                  start=(d == 0), stop=(d == 7))
                                mm.then_inc(sT.h, 1); sT.inc()
                                for d in range(8):
                                    mm = t.matmul(ps_h3[b][:, :],
                                                  w3s[:, fb % 2, d, fi * 128:fi * 128 + 128],
                                                  XH[:, d, 8 * h + 4 * u:8 * h + 4 * u + 4, :],
                                                  start=(d == 0), stop=(d == 7))
                                mm.then_inc(sT.h, 1); sT.inc()

                        @block.scalar
                        def _(s, T3=T3, V3=V3):
                            for i, (f, u) in enumerate(iters):
                                b = i % 2
                                s.wait_ge(sT.h, T3 + 2 * i + 1)
                                if i >= 2:
                                    s.wait_ge(sV.h, V3 + 2 * (i - 2) + 2)
                                s.activation(gel[:, b, :], ps_h1[b][:, :], AF.Gelu).then_inc(sA.h, 1); sA.inc()

                        @block.vector
                        def _(v, e=e, h=h, T3=T3, A3=A3):
                            for i, (f, u) in enumerate(iters):
                                b = i % 2
                                v.wait_ge(sT.h, T3 + 2 * i + 2)
                                if e < 8:
                                    ins = v.tensor_tensor(h3s[:, b, :], ps_h3[b][:, :],
                                                          CWb[:, h * 1024 + u * 512:h * 1024 + u * 512 + 512],
                                                          ALU.mult)
                                else:
                                    ins = v.tensor_copy(h3s[:, b, :], ps_h3[b][:, :])
                                ins.then_inc(sV.h, 1); sV.inc()
                                v.wait_ge(sA.h, A3 + i + 1)
                                v.tensor_tensor(H[:, f, u * 512:(u + 1) * 512], gel[:, b, :],
                                                h3s[:, b, :], ALU.mult).then_inc(sV.h, 1); sV.inc()
                        ph1_done_V = sV.n

                        T4, V4 = sT.n, sV.n
                        jiters = [(mq, tc) for mq in range(4) for tc in range(8)]

                        @block.tensor
                        def _(t, e=e, h=h, T4=T4, V4=V4, ph1_done_V=ph1_done_V):
                            t.wait_ge(sV.h, ph1_done_V)
                            for j, (mq, tc) in enumerate(jiters):
                                if tc == 0:
                                    t.wait_ge(sW.h, w2_tgt[(e, h, mq)])
                                if j >= 2:
                                    t.wait_ge(sV.h, V4 + (j - 1))
                                b = j % 2
                                for f in range(FC):
                                    mm = t.matmul(ps_y[b][:, :], H[:, f, tc * 128:(tc + 1) * 128],
                                                  w2s[:, mq % 2, f, :], start=(f == 0), stop=(f == FC - 1))
                                mm.then_inc(sT.h, 1); sT.inc()

                        @block.vector
                        def _(v, h=h, T4=T4):
                            for j, (mq, tc) in enumerate(jiters):
                                b = j % 2
                                v.wait_ge(sT.h, T4 + j + 1)
                                v.scalar_tensor_tensor(RES[:, 8 * h + tc, mq * 256:(mq + 1) * 256],
                                                       ps_y[b][:, :], 0.5,
                                                       RES[:, 8 * h + tc, mq * 256:(mq + 1) * 256],
                                                       ALU.mult, ALU.add).then_inc(sV.h, 1); sV.inc()

                assert sT.n == t_virtual, (sT.n, t_virtual)
                assert sV.n == v_virtual, (sV.n, v_virtual)
            return sT.n, sV.n


        def emit_attention(stage):
            logk = float(np.log(1152.0))
            with ExitStack() as at:
                xnT = sb("xnT", [128, 8, 18, 128], BF16, at)
                maskr = sb("maskr", [128, 2, 32], F32, at)
                rankr = sb("rankr", [128, 2, 32], F32, at)
                idxq = sb("idxq", [128, 2, 8], U32, at)
                idx2q = sb("idx2q", [128, 2, 8], U32, at)
                scores = sb("scores", [128, 64], F32, at)
                hgamb = sb("hgamb", [128, D], F32, at)
                ps_a = [mkpsum(f"psa{i}", [128, 512], F32, at) for i in range(2)]
                ps_b = [mkpsum(f"psb{i}", [128, 512], F32, at) for i in range(2)]
                ps_z = mkpsum("psz", [1, 512], F32, at)
                ps_i4 = mkpsum("psi4", [4, 512], F32, at)
                ps_s = [mkpsum(f"pss{i}", [128, 512], F32, at) for i in range(2)]

                VA, AA, TA, GA, WA, SA_ = sV.n, sA.n, sT.n, sG.n, sW.n, sSc.n

                with ExitStack() as a0:
                    tokb = sb("tokb", [128, 2, D], F32, a0)
                    gamlb = sb("gamlb", [128, D], F32, a0)
                    betlb = sb("betlb", [128, D], F32, a0)
                    zrot = sb("zrot", [128, 2, D], F32, a0)
                    xnrot = sb("xnrot", [128, 2, D], BF16, a0)
                    asqt = sb("asqt", [128, D], F32, a0)
                    stmp = sb("stmp", [128, D], F32, a0)
                    astat = sb("astat", [128, 16], F32, a0)
                    s_loc = sb("s_loc", [128, 2, 16], F32, a0)
                    s_row2 = sb("s_row2", [2, 2048], F32, a0)
                    zwi = sb("zwi", [128, 16], U32, a0)
                    haloi = sb("haloi", [128, 2], U32, a0)
                    mgt = sb("mgt", [2, 16], I32, a0)
                    pfl = sb("pfl", [2, 16], I32, a0)
                    fwi = sb("fwi", [2, 1], U32, a0)
                    pli = sb("pli", [2, 1], U32, a0)
                    sri = sb("sri", [2, 1], U32, a0)
                    s_sb = sb("s_sb", [128, 64], F32, a0)
                    u_t = sb("u_t", [128, 64], F32, a0)
                    e_t = sb("e_t", [128, 64], F32, a0)
                    rm2 = sb("rm2", [128, 2], F32, a0)
                    bcb = sb("bcb", [128, 2], F32, a0)
                    scs = sb("scs", [1, 32], F32, a0)
                    cs32 = sb("cs32", [1, 32], F32, a0)
                    za32 = sb("za32", [1, 32], F32, a0)
                    tri_s = sb("tri_s", [128, 32], F32, a0)
                    cum_s = sb("cum_s", [128, 32], F32, a0)
                    msk_s = sb("msk_s", [128, 2, 32], F32, a0)
                    Pf = sb("Pf", [128, 2, 1024], BF16, a0)
                    idxs_s = sb("idxs_s", [4, 1024], F32, a0)
                    itp = sb("itp", [128, 4], F32, a0)
                    iv_f = sb("iv_f", [128, 2], F32, a0)
                    iotar_sb = sb("iotar_sb", [128, 1024], F32, a0)
                    iotaPb = sb("iotaPb", [128, 32, 4], BF16, a0)
                    screp = sb("screp", [128, 32, 16], F32, a0)

                    grow1 = sb("grow1", [1, D], F32, a0)

                    @block.sync
                    def _(sync):
                        sync.wait_ge(sT.h, TA)
                        sync.wait_ge(sV.h, VA)
                        sync.dma_start(out=s_row2[0:1, :], in_=rtoks[:, :]).then_inc(sD.h, 16); sD.inc(16)
                        for j, src in enumerate([ln_g, ln_b, hgam]):
                            sync.wait_ge(sV.h, VA + 4 + 2 * j)
                            sync.dma_start(out=grow1[:, :], in_=src[:, :]).then_inc(sD.h, 16); sD.inc(16)
                    sD_rows0 = sD.n - 64

                    @block.tensor
                    def _(t):
                        t.wait_ge(sD.h, sD_rows0 + 16)
                        t.wait_ge(sD.h, sD_pre)
                        t.wait_ge(sV.h, VA)
                        for j in range(4):
                            t.matmul(ps_s[0][:, 0:512], onesrow[:, :], s_row2[0:1, 1024 * (j // 2) + 512 * (j % 2):
                                                                              1024 * (j // 2) + 512 * (j % 2) + 512],
                                     start=True, stop=True).then_inc(sT.h, 1); sT.inc()
                            t.wait_ge(sV.h, VA + j + 1)
                        for j in range(6):
                            if j % 2 == 0:
                                t.wait_ge(sD.h, sD_rows0 + 16 * (j // 2 + 2))
                            t.matmul(ps_s[0][:, 0:512], onesrow[:, :], grow1[0:1, 512 * (j % 2):512 * (j % 2) + 512],
                                     start=True, stop=True).then_inc(sT.h, 1); sT.inc()
                            t.wait_ge(sV.h, VA + 4 + j + 1)

                    @block.vector
                    def _(v):
                        dsts = [tokb[:, 0, 0:512], tokb[:, 0, 512:1024], tokb[:, 1, 0:512], tokb[:, 1, 512:1024],
                                gamlb[:, 0:512], gamlb[:, 512:1024], betlb[:, 0:512], betlb[:, 512:1024],
                                hgamb[:, 0:512], hgamb[:, 512:1024]]
                        for j, dv in enumerate(dsts):
                            v.wait_ge(sT.h, TA + j + 1)
                            v.tensor_copy(dv, ps_s[0][:, 0:512]).then_inc(sV.h, 1); sV.inc()
                    VA0 = sV.n
                    TA0 = sT.n
                    assert VA0 == VA + 10 and TA0 == TA + 10

                    @block.sync
                    def _(sync):
                        sync.wait_ge(sT.h, TA)
                        sync.wait_ge(sV.h, VA)
                        sync.dma_start(out=zwi[:, :], in_=zw_idx_c[:, :]).then_inc(sD.h, 16); sD.inc(16)
                        sync.dma_start(out=haloi[:, :], in_=halo_idx_c[:, :]).then_inc(sD.h, 16); sD.inc(16)
                        sync.dma_start(out=mgt[0:2, :], in_=magic_c[0:1, :].broadcast_to((2, 16))).then_inc(sD.h, 16); sD.inc(16)
                        sync.dma_start(out=fwi[0:2, 0:1], in_=fwr_idx_c[:, None]).then_inc(sD.h, 16); sD.inc(16)
                        sync.dma_start(out=pli[0:2, 0:1], in_=poll_idx_c[:, :]).then_inc(sD.h, 16); sD.inc(16)
                        sync.dma_start(out=sri[0:2, 0:1], in_=srow_idx_c[:, None]).then_inc(sD.h, 16); sD.inc(16)
                        sync.dma_start(out=iotar_sb[:, :], in_=iotar_c[:, :]).then_inc(sD.h, 16); sD.inc(16)
                        sync.dma_start(out=iotaPb[:, :, :], in_=iotaP_c[:, :, :]).then_inc(sD.h, 16); sD.inc(16)
                    sD_A = sD.n

                    V1_, A1_, S1_, W1_ = sV.n, sA.n, sSc.n, sW.n

                    @block.vector
                    def _(v):
                        for c in range(C16):
                            r = c % 2
                            so = r * 8
                            if c >= 2:
                                v.wait_ge(sA.h, A1_ + 2 * (c - 1))
                                v.wait_ge(sW.h, W1_ + 16 * (c - 1))
                                v.wait_ge(sSc.h, S1_ + 16 * (c - 1))
                            v.tensor_reduce(astat[:, so:so + 1], RES[:, c, :], axis=AX.X, op=ALU.add)
                            v.drain()
                            v.tensor_scalar(astat[:, so + 1:so + 2], astat[:, so:so + 1], -1.0 / D,
                                            None, ALU.mult).then_inc(sV.h, 1); sV.inc()
                            v.wait_ge(sA.h, A1_ + 2 * c + 1)
                            v.drain()
                            v.tensor_scalar(zrot[:, r, :], RES[:, c, :], astat[:, so + 1:so + 2],
                                            astat[:, so + 3:so + 4], ALU.add, ALU.mult).then_inc(sV.h, 1); sV.inc()
                            for rho in range(2):
                                v.tensor_tensor(stmp[:, :], zrot[:, r, :], tokb[:, rho, :], ALU.mult)
                                v.tensor_reduce(s_loc[:, rho, c:c + 1], stmp[:, :], axis=AX.X, op=ALU.add)
                            v.tensor_reduce(astat[:, so + 4:so + 5], zrot[:, r, :], axis=AX.X, op=ALU.add)
                            v.drain()
                            v.tensor_scalar(astat[:, so + 5:so + 6], astat[:, so + 4:so + 5], -1.0 / D,
                                            None, ALU.mult).then_inc(sV.h, 1); sV.inc()
                            v.wait_ge(sA.h, A1_ + 2 * c + 2)
                            v.drain()
                            v.tensor_scalar(stmp[:, :], zrot[:, r, :], astat[:, so + 5:so + 6],
                                            astat[:, so + 7:so + 8], ALU.add, ALU.mult)
                            v.tensor_tensor(stmp[:, :], stmp[:, :], gamlb[:, :], ALU.mult)
                            v.tensor_tensor(xnrot[:, r, :], stmp[:, :], betlb[:, :], ALU.add).then_inc(sV.h, 1); sV.inc()
                    assert sV.n == V1_ + 4 * C16

                    @block.scalar
                    def _(s):
                        for c in range(C16):
                            r = c % 2
                            so = r * 8
                            s.wait_ge(sV.h, V1_ + 4 * c + 1)
                            s.activation(asqt[:, :], RES[:, c, :], AF.Square,
                                         bias=astat[:, so + 1:so + 2], accum_out=astat[:, so + 2:so + 3])
                            s.drain()
                            s.mul(astat[:, so + 6:so + 7], astat[:, so + 2:so + 3], 1.0 / D)
                            s.drain()
                            s.activation(astat[:, so + 6:so + 7], astat[:, so + 6:so + 7], AF.Ln,
                                         bias=consts[:, 1:2])
                            s.drain()
                            s.activation(astat[:, so + 3:so + 4], astat[:, so + 6:so + 7], AF.Exp,
                                         scale=-0.5).then_inc(sA.h, 1); sA.inc()
                            s.wait_ge(sV.h, V1_ + 4 * c + 3)
                            s.activation(asqt[:, :], zrot[:, r, :], AF.Square,
                                         bias=astat[:, so + 5:so + 6], accum_out=astat[:, so + 2:so + 3])
                            s.drain()
                            s.mul(astat[:, so + 6:so + 7], astat[:, so + 2:so + 3], 1.0 / D)
                            s.drain()
                            s.activation(astat[:, so + 6:so + 7], astat[:, so + 6:so + 7], AF.Ln,
                                         bias=consts[:, 1:2])
                            s.drain()
                            s.activation(astat[:, so + 7:so + 8], astat[:, so + 6:so + 7], AF.Exp,
                                         scale=-0.5).then_inc(sA.h, 1); sA.inc()
                            s.wait_ge(sV.h, V1_ + 4 * c + 4)
                            s.dma_start_transpose(xnT[:, :, c + 1, :], xnrot[:, r, :]).then_inc(sSc.h, 16); sSc.inc(16)
                    assert sA.n == A1_ + 2 * C16 and sSc.n == S1_ + 16 * C16

                    @block.gpsimd
                    def _(g):
                        g.wait_ge(sD.h, sD_A)
                        for c in range(C16):
                            r = c % 2
                            g.wait_ge(sV.h, V1_ + 4 * c + 2)
                            g.indirect_dma_start(
                                out=z_pair.ap(), out_offset=IndirectOffsetOnAxis(zwi[:, c:c + 1], axis=0),
                                in_=zrot[:, r, :], in_offset=None).then_inc(sW.h, 16); sW.inc(16)
                    assert sW.n == W1_ + 16 * C16

                    VA1, AA1, WA1, GA1, SA1 = sV.n, sA.n, sW.n, sG.n, sSc.n

                    @block.gpsimd
                    def _(g):
                        g.wait_ge(sV.h, VA1)
                        g.dma_start(out=s_row2[0:1, :], in_=s_loc[:, 0, :]).then_inc(sW.h, 16); sW.inc(16)
                        g.dma_start(out=s_row2[1:2, :], in_=s_loc[:, 1, :]).then_inc(sW.h, 16); sW.inc(16)
                        g.wait_ge(sW.h, sW.n)
                        g.indirect_dma_start(
                            out=s_pair.ap(), out_offset=IndirectOffsetOnAxis(sri[0:2, 0:1], axis=0),
                            in_=s_row2[0:2, :], in_offset=None).then_inc(sW.h, 16); sW.inc(16)
                        g.wait_ge(sW.h, sW.n)
                        g.indirect_dma_start(
                            out=f_pair.ap(), out_offset=IndirectOffsetOnAxis(fwi[0:2, 0:1], axis=0),
                            in_=mgt[0:2, :], in_offset=None).then_inc(sW.h, 16); sW.inc(16)
                        g.wait_ge(sW.h, sW.n)
                        with (
                            g.register("fr") as fr,
                            g.register("mr") as mr,
                        ):
                            g.reg_load(mr, mgt[0:1, 0:1])
                            g.reg_mov(fr, 0)
                            cnt0 = sB.n
                            for i in range(POLLS):
                                with g.If_ne(fr, mr):
                                    g.indirect_dma_start(
                                        out=pfl[0:2, :], out_offset=None,
                                        in_=f_pair.ap(),
                                        in_offset=IndirectOffsetOnAxis(pli[0:2, 0:1], axis=0)).then_inc(sB.h, 16)
                                    cnt0 += 16
                                    g.wait_ge(sB.h, cnt0)
                                    g.reg_load(fr, pfl[0:1, 0:1])
                            sB.n = cnt0
                        for hc in range(2):
                            g.indirect_dma_start(
                                out=zrot[:, hc, :], out_offset=None,
                                in_=z_pair.ap(),
                                in_offset=IndirectOffsetOnAxis(haloi[:, hc:hc + 1], axis=0)).then_inc(sW.h, 16); sW.inc(16)
                        g.wait_ge(sW.h, sW.n)
                        g.tensor_copy(pfl[0:2, 8:16], mgt[0:2, 0:8]).then_inc(sG.h, 1); sG.inc()
                    GA_bar = sG.n

                    VH, AH = sV.n, sA.n

                    @block.vector
                    def _(v):
                        v.wait_ge(sG.h, GA_bar)
                        for hc in range(2):
                            so = hc * 8
                            v.tensor_reduce(astat[:, so + 4:so + 5], zrot[:, hc, :], axis=AX.X, op=ALU.add)
                            v.drain()
                            v.tensor_scalar(astat[:, so + 5:so + 6], astat[:, so + 4:so + 5], -1.0 / D,
                                            None, ALU.mult).then_inc(sV.h, 1); sV.inc()
                            v.wait_ge(sA.h, AH + hc + 1)
                            v.drain()
                            v.tensor_scalar(stmp[:, :], zrot[:, hc, :], astat[:, so + 5:so + 6],
                                            astat[:, so + 7:so + 8], ALU.add, ALU.mult)
                            v.tensor_tensor(stmp[:, :], stmp[:, :], gamlb[:, :], ALU.mult)
                            v.tensor_tensor(xnrot[:, hc, :], stmp[:, :], betlb[:, :], ALU.add).then_inc(sV.h, 1); sV.inc()

                    @block.scalar
                    def _(s):
                        for hc in range(2):
                            so = hc * 8
                            s.wait_ge(sV.h, VH + 2 * hc + 1)
                            s.activation(asqt[:, :], zrot[:, hc, :], AF.Square,
                                         bias=astat[:, so + 5:so + 6], accum_out=astat[:, so + 2:so + 3])
                            s.drain()
                            s.mul(astat[:, so + 6:so + 7], astat[:, so + 2:so + 3], 1.0 / D)
                            s.drain()
                            s.activation(astat[:, so + 6:so + 7], astat[:, so + 6:so + 7], AF.Ln,
                                         bias=consts[:, 1:2])
                            s.drain()
                            s.activation(astat[:, so + 7:so + 8], astat[:, so + 6:so + 7], AF.Exp,
                                         scale=-0.5).then_inc(sA.h, 1); sA.inc()
                            s.wait_ge(sV.h, VH + 2 * hc + 2)
                            s.dma_start_transpose(xnT[:, :, 17 * hc, :], xnrot[:, hc, :]).then_inc(sSc.h, 16); sSc.inc(16)
                    xnT_S_done = sSc.n

                    @block.sync
                    def _(sync):
                        sync.wait_ge(sG.h, GA_bar)
                        for rho in range(2):
                            for hh in range(2):
                                sync.dma_start(out=s_sb[:, rho * 32 + hh * 16: rho * 32 + hh * 16 + 16],
                                               in_=s_pair.ap()[2 * hh + rho, :].rearrange("(p c) -> p c", p=128)
                                               ).then_inc(sD.h, 16); sD.inc(16)
                    sD_s = sD.n

                    VC, AC, TC = sV.n, sA.n, sT.n

                    @block.vector
                    def _(v):
                        v.wait_ge(sD.h, sD_s)
                        v.memset(u_t[:, :], 0.0)
                        v.memset(za32[:, :], 0.0)
                        v.nop()
                        v.tensor_copy(scores[:, 0:1], u_t[:, 0:1]).then_inc(sV.h, 1); sV.inc()

                    cur = 4.0
                    vc = VC + 1
                    ac, tc_ = AC, TC
                    for it in range(20):
                        for rho in range(2):
                            cs_ = rho * 32
                            @block.vector
                            def _(v, rho=rho, cs_=cs_, vc=vc, tc_=tc_, cur=cur):
                                v.wait_ge(sT.h, tc_)
                                v.tensor_reduce(rm2[:, rho:rho + 1], u_t[:, cs_:cs_ + 32], axis=AX.X,
                                                op=ALU.max).then_inc(sV.h, 1); sV.inc()

                            @block.tensor
                            def _(t, rho=rho, vc=vc):
                                t.wait_ge(sV.h, vc + 1)
                                t.transpose(ps_s[0][0:1, 0:128], rm2[:, rho:rho + 1], identF[:, :]).then_inc(sT.h, 1); sT.inc()

                            @block.vector
                            def _(v, rho=rho, tc_=tc_, cur=cur):
                                v.wait_ge(sT.h, tc_ + 1)
                                v.tensor_reduce(scs[0:1, 0:1], ps_s[0][0:1, 0:128], axis=AX.X, op=ALU.max)
                                v.drain()
                                v.tensor_scalar(scs[0:1, 1:2], scs[0:1, 0:1], -1.0 / cur, None, ALU.mult).then_inc(sV.h, 1); sV.inc()

                            @block.tensor
                            def _(t, vc=vc):
                                t.wait_ge(sV.h, vc + 2)
                                t.matmul(ps_s[1][:, 0:1], onesrow[:, :], scs[0:1, 1:2], start=True, stop=True).then_inc(sT.h, 1); sT.inc()

                            @block.vector
                            def _(v, tc_=tc_):
                                v.wait_ge(sT.h, tc_ + 2)
                                v.tensor_copy(bcb[:, 0:1], ps_s[1][:, 0:1]).then_inc(sV.h, 1); sV.inc()

                            @block.scalar
                            def _(s, rho=rho, cs_=cs_, vc=vc, cur=cur):
                                s.wait_ge(sV.h, vc + 3)
                                s.activation(e_t[:, cs_:cs_ + 32], u_t[:, cs_:cs_ + 32], AF.Exp,
                                             scale=1.0 / cur, bias=bcb[:, 0:1]).then_inc(sA.h, 1); sA.inc()

                            @block.vector
                            def _(v, rho=rho, cs_=cs_, ac=ac, vc=vc):
                                v.wait_ge(sA.h, ac + 1)
                                v.tensor_reduce(rm2[:, rho:rho + 1], e_t[:, cs_:cs_ + 32], axis=AX.X,
                                                op=ALU.add).then_inc(sV.h, 1); sV.inc()

                            @block.tensor
                            def _(t, rho=rho, vc=vc):
                                t.wait_ge(sV.h, vc + 4)
                                t.transpose(ps_s[0][0:1, 0:128], rm2[:, rho:rho + 1], identF[:, :]).then_inc(sT.h, 1); sT.inc()

                            @block.vector
                            def _(v, tc_=tc_):
                                v.wait_ge(sT.h, tc_ + 3)
                                v.tensor_reduce(scs[0:1, 2:3], ps_s[0][0:1, 0:128], axis=AX.X,
                                                op=ALU.add).then_inc(sV.h, 1); sV.inc()

                            @block.scalar
                            def _(s, vc=vc):
                                s.wait_ge(sV.h, vc + 5)
                                s.activation(scs[0:1, 3:4], scs[0:1, 2:3], AF.Ln).then_inc(sA.h, 1); sA.inc()

                            @block.vector
                            def _(v, ac=ac, cur=cur, cs_=cs_, rho=rho):
                                v.wait_ge(sA.h, ac + 2)
                                v.drain()
                                v.tensor_scalar(scs[0:1, 4:5], scs[0:1, 3:4], -cur, cur * logk, ALU.mult, ALU.add)
                                v.drain()
                                v.tensor_scalar(scs[0:1, 5:6], scs[0:1, 1:2], cur, None, ALU.mult)
                                v.drain()
                                v.tensor_add(scs[0:1, 6:7], scs[0:1, 4:5], scs[0:1, 5:6])
                                v.drain()
                                v.tensor_scalar(scs[0:1, 7:8], scs[0:1, 6:7], -1.0, None, ALU.mult).then_inc(sV.h, 1); sV.inc()

                            @block.tensor
                            def _(t, vc=vc):
                                t.wait_ge(sV.h, vc + 6)
                                t.matmul(ps_s[1][:, 1:2], onesrow[:, :], scs[0:1, 7:8], start=True, stop=True).then_inc(sT.h, 1); sT.inc()

                            @block.vector
                            def _(v, cs_=cs_, tc_=tc_, rho=rho):
                                v.wait_ge(sT.h, tc_ + 4)
                                v.tensor_scalar(u_t[:, cs_:cs_ + 32], s_sb[:, cs_:cs_ + 32],
                                                ps_s[1][:, 1:2], None, ALU.min).then_inc(sV.h, 1); sV.inc()
                                if it == 19:
                                    v.wait_ge(sT.h, tc_ + 5)
                                    v.drain()
                                    v.tensor_scalar(e_t[:, cs_:cs_ + 32], u_t[:, cs_:cs_ + 32],
                                                    ps_s[1][:, 2:3], None, ALU.add).then_inc(sV.h, 1); sV.inc()

                            if it == 19:
                                @block.tensor
                                def _(t, vc=vc):
                                    t.wait_ge(sV.h, vc + 6)
                                    t.matmul(ps_s[1][:, 2:3], onesrow[:, :], scs[0:1, 6:7], start=True, stop=True).then_inc(sT.h, 1); sT.inc()

                                @block.scalar
                                def _(s, cs_=cs_, vc=vc):
                                    s.wait_ge(sV.h, vc + 8)
                                    s.activation(scores[:, cs_:cs_ + 32], e_t[:, cs_:cs_ + 32], AF.Exp,
                                                 scale=1.0 / 0.03).then_inc(sA.h, 1); sA.inc()
                                vc += 8
                                tc_ += 5
                                ac += 3
                            else:
                                vc += 7
                                tc_ += 4
                                ac += 2
                        cur = max(cur * 0.7, 0.03)
                    assert sV.n == vc and sT.n == tc_ and sA.n == ac, (sV.n, vc, sT.n, tc_, sA.n, ac)

                    for rho in range(2):
                        cs_ = rho * 32
                        VS, AS_, TS_, WS_ = sV.n, sA.n, sT.n, sW.n

                        @block.vector
                        def _(v, cs_=cs_, TS_=TS_, AS_=AS_):
                            v.wait_ge(sA.h, AS_)
                            v.drain()
                            v.tensor_reduce(rm2[:, 0:1], scores[:, cs_:cs_ + 32], axis=AX.X, op=ALU.max).then_inc(sV.h, 1); sV.inc()

                        @block.tensor
                        def _(t, VS=VS):
                            t.wait_ge(sV.h, VS + 1)
                            t.transpose(ps_s[0][0:1, 0:128], rm2[:, 0:1], identF[:, :]).then_inc(sT.h, 1); sT.inc()

                        @block.vector
                        def _(v, TS_=TS_, cs_=cs_):
                            v.wait_ge(sT.h, TS_ + 1)
                            v.tensor_reduce(scs[0:1, 8:9], ps_s[0][0:1, 0:128], axis=AX.X, op=ALU.max)
                            v.drain()
                            v.tensor_scalar(scs[0:1, 10:11], scs[0:1, 8:9], 1.0001, 1e-30, ALU.mult, ALU.add)
                            v.memset(scs[0:1, 9:10], 0.0)
                            v.drain().then_inc(sV.h, 1); sV.inc()

                        vv_, tt_ = VS + 2, TS_ + 1
                        for bi in range(22):
                            @block.vector
                            def _(v, vv_=vv_, tt_=tt_, cs_=cs_, bi=bi):
                                if bi > 0:
                                    v.wait_ge(sT.h, tt_)
                                    v.drain()
                                    v.tensor_scalar(scs[0:1, 12:13], ps_z[0:1, 0:1], 1024.0, None, ALU.is_ge)
                                    v.drain()
                                    v.tensor_sub(scs[0:1, 13:14], scs[0:1, 11:12], scs[0:1, 9:10])
                                    v.tensor_sub(scs[0:1, 14:15], scs[0:1, 11:12], scs[0:1, 10:11])
                                    v.drain()
                                    v.tensor_tensor(scs[0:1, 13:14], scs[0:1, 13:14], scs[0:1, 12:13], ALU.mult)
                                    v.tensor_scalar(scs[0:1, 15:16], scs[0:1, 12:13], -1.0, 1.0, ALU.mult, ALU.add)
                                    v.drain()
                                    v.tensor_tensor(scs[0:1, 14:15], scs[0:1, 14:15], scs[0:1, 15:16], ALU.mult)
                                    v.drain()
                                    v.tensor_add(scs[0:1, 9:10], scs[0:1, 9:10], scs[0:1, 13:14])
                                    v.tensor_add(scs[0:1, 10:11], scs[0:1, 10:11], scs[0:1, 14:15])
                                    v.drain()
                                v.tensor_add(scs[0:1, 11:12], scs[0:1, 9:10], scs[0:1, 10:11])
                                v.drain()
                                v.tensor_scalar(scs[0:1, 11:12], scs[0:1, 11:12], 0.5, None, ALU.mult).then_inc(sV.h, 1); sV.inc()

                            @block.tensor
                            def _(t, vv_=vv_):
                                t.wait_ge(sV.h, vv_ + 1)
                                t.matmul(ps_s[1][:, 3:4], onesrow[:, :], scs[0:1, 11:12], start=True, stop=True).then_inc(sT.h, 1); sT.inc()

                            @block.vector
                            def _(v, tt_=tt_, cs_=cs_):
                                v.wait_ge(sT.h, tt_ + 1)
                                v.tensor_scalar(msk_s[:, 0, :], scores[:, cs_:cs_ + 32], ps_s[1][:, 3:4],
                                                None, ALU.is_ge)
                                v.drain()
                                v.tensor_reduce(rm2[:, 0:1], msk_s[:, 0, :], axis=AX.X, op=ALU.add).then_inc(sV.h, 1); sV.inc()

                            @block.tensor
                            def _(t, vv_=vv_):
                                t.wait_ge(sV.h, vv_ + 2)
                                t.matmul(ps_z[0:1, 0:1], onescol[:, :], rm2[:, 0:1], start=True, stop=True).then_inc(sT.h, 1); sT.inc()
                            vv_ += 2
                            tt_ += 2
                        assert sV.n == vv_ and sT.n == tt_

                        @block.vector
                        def _(v, vv_=vv_, tt_=tt_, cs_=cs_, rho=rho):
                            v.wait_ge(sT.h, tt_)
                            v.drain()
                            v.tensor_scalar(scs[0:1, 12:13], ps_z[0:1, 0:1], 1024.0, None, ALU.is_ge)
                            v.drain()
                            v.tensor_sub(scs[0:1, 13:14], scs[0:1, 11:12], scs[0:1, 9:10])
                            v.tensor_sub(scs[0:1, 14:15], scs[0:1, 11:12], scs[0:1, 10:11])
                            v.drain()
                            v.tensor_tensor(scs[0:1, 13:14], scs[0:1, 13:14], scs[0:1, 12:13], ALU.mult)
                            v.tensor_scalar(scs[0:1, 15:16], scs[0:1, 12:13], -1.0, 1.0, ALU.mult, ALU.add)
                            v.drain()
                            v.tensor_tensor(scs[0:1, 14:15], scs[0:1, 14:15], scs[0:1, 15:16], ALU.mult)
                            v.drain()
                            v.tensor_add(scs[0:1, 9:10], scs[0:1, 9:10], scs[0:1, 13:14])
                            v.tensor_add(scs[0:1, 10:11], scs[0:1, 10:11], scs[0:1, 14:15]).then_inc(sV.h, 1); sV.inc()
                        vv_ += 1

                        @block.tensor
                        def _(t, vv_=vv_):
                            t.wait_ge(sV.h, vv_)
                            t.matmul(ps_s[1][:, 3:4], onesrow[:, :], scs[0:1, 10:11], start=True, stop=True)
                            t.matmul(ps_s[1][:, 4:5], onesrow[:, :], scs[0:1, 9:10], start=True, stop=True).then_inc(sT.h, 1); sT.inc()
                        tt_ += 1

                        @block.vector
                        def _(v, tt_=tt_, cs_=cs_, rho=rho):
                            v.wait_ge(sT.h, tt_)
                            v.tensor_scalar(msk_s[:, 0, :], scores[:, cs_:cs_ + 32], ps_s[1][:, 3:4],
                                            None, ALU.is_ge)
                            v.tensor_scalar(msk_s[:, 1, :], scores[:, cs_:cs_ + 32], ps_s[1][:, 4:5],
                                            None, ALU.is_ge)
                            v.drain()
                            v.tensor_sub(msk_s[:, 1, :], msk_s[:, 1, :], msk_s[:, 0, :])
                            v.drain()
                            v.tensor_reduce(rm2[:, 0:1], msk_s[:, 0, :], axis=AX.X, op=ALU.add).then_inc(sV.h, 1); sV.inc()
                        vv_ += 1

                        @block.tensor
                        def _(t, vv_=vv_):
                            t.wait_ge(sV.h, vv_)
                            t.matmul(ps_z[0:1, 0:1], onescol[:, :], rm2[:, 0:1], start=True, stop=True)
                            t.matmul(ps_s[0][:, 0:32], triI[:, :], msk_s[:, 1, :], start=True, stop=True)
                            t.matmul(ps_z[0:1, 1:33], onescol[:, :], msk_s[:, 1, :], start=True, stop=True).then_inc(sT.h, 1); sT.inc()
                        tt_ += 1

                        @block.vector
                        def _(v, tt_=tt_):
                            v.wait_ge(sT.h, tt_)
                            v.tensor_copy(cs32[0:1, :], ps_z[0:1, 1:33])
                            v.tensor_copy(tri_s[:, :], ps_s[0][:, 0:32])
                            v.tensor_scalar(scs[0:1, 16:17], ps_z[0:1, 0:1], -1.0, 1024.0, ALU.mult, ALU.add)
                            v.drain()
                            v.tensor_tensor_scan(cs32[0:1, :], cs32[0:1, :], za32[0:1, :], 0.0, ALU.add, ALU.add)
                            v.drain()
                            v.tensor_sub(cs32[0:1, :], cs32[0:1, :], ps_z[0:1, 1:33]).then_inc(sV.h, 1); sV.inc()
                        vv_ += 1

                        @block.tensor
                        def _(t, vv_=vv_):
                            t.wait_ge(sV.h, vv_)
                            t.matmul(ps_s[1][:, 5:37], onesrow[:, :], cs32[0:1, :], start=True, stop=True)
                            t.matmul(ps_s[1][:, 37:38], onesrow[:, :], scs[0:1, 16:17], start=True, stop=True).then_inc(sT.h, 1); sT.inc()
                        tt_ += 1

                        @block.vector
                        def _(v, tt_=tt_, cs_=cs_, rho=rho):
                            v.wait_ge(sT.h, tt_)
                            v.tensor_tensor(cum_s[:, :], tri_s[:, :], ps_s[1][:, 5:37], ALU.add)
                            v.drain()
                            v.tensor_scalar(cum_s[:, :], cum_s[:, :], ps_s[1][:, 37:38], None, ALU.is_le)
                            v.drain()
                            v.tensor_tensor(cum_s[:, :], cum_s[:, :], msk_s[:, 1, :], ALU.mult)
                            v.drain()
                            v.tensor_add(maskr[:, rho, :], msk_s[:, 0, :], cum_s[:, :]).then_inc(sV.h, 1); sV.inc()
                        vv_ += 1

                        @block.tensor
                        def _(t, vv_=vv_, rho=rho):
                            t.wait_ge(sV.h, vv_)
                            t.matmul(ps_s[0][:, 0:32], triI[:, :], maskr[:, rho, :], start=True, stop=True)
                            t.matmul(ps_z[0:1, 1:33], onescol[:, :], maskr[:, rho, :], start=True, stop=True).then_inc(sT.h, 1); sT.inc()
                        tt_ += 1

                        @block.vector
                        def _(v, tt_=tt_):
                            v.wait_ge(sT.h, tt_)
                            v.tensor_copy(cs32[0:1, :], ps_z[0:1, 1:33])
                            v.tensor_copy(tri_s[:, :], ps_s[0][:, 0:32])
                            v.drain()
                            v.tensor_tensor_scan(cs32[0:1, :], cs32[0:1, :], za32[0:1, :], 0.0, ALU.add, ALU.add)
                            v.drain()
                            v.tensor_sub(cs32[0:1, :], cs32[0:1, :], ps_z[0:1, 1:33]).then_inc(sV.h, 1); sV.inc()
                        vv_ += 1

                        @block.tensor
                        def _(t, vv_=vv_):
                            t.wait_ge(sV.h, vv_)
                            t.matmul(ps_s[1][:, 5:37], onesrow[:, :], cs32[0:1, :], start=True, stop=True).then_inc(sT.h, 1); sT.inc()
                        tt_ += 1

                        @block.vector
                        def _(v, tt_=tt_, rho=rho):
                            v.wait_ge(sT.h, tt_)
                            v.tensor_tensor(rankr[:, rho, :], tri_s[:, :], ps_s[1][:, 5:37], ALU.add).then_inc(sV.h, 1); sV.inc()
                        vv_ += 1
                        assert sV.n == vv_ and sT.n == tt_

                        VP, TP = sV.n, sT.n

                        @block.vector
                        def _(v, rho=rho, VP=VP, TP=TP):
                            v.drain()
                            v.tensor_scalar(cum_s[:, :], rankr[:, rho, :], 1.0, -1.0, ALU.mult, ALU.add)
                            v.drain()
                            v.tensor_tensor(cum_s[:, :], cum_s[:, :], maskr[:, rho, :], ALU.mult)
                            v.drain()
                            v.tensor_scalar(tri_s[:, :], maskr[:, rho, :], 5.0, -5.0, ALU.mult, ALU.add)
                            v.drain()
                            v.tensor_add(cum_s[:, :], cum_s[:, :], tri_s[:, :])
                            v.drain().then_inc(sV.h, 1); sV.inc()
                            for f in range(32):
                                if f >= 2:
                                    v.wait_ge(sT.h, TP + (f - 1))
                                v.tensor_scalar(Pf[:, f % 2, :], iotar_sb[:, :], cum_s[:, f:f + 1],
                                                None, ALU.is_equal).then_inc(sV.h, 1); sV.inc()

                        @block.tensor
                        def _(t, VP=VP, rho=rho):
                            for f in range(32):
                                t.wait_ge(sV.h, VP + 1 + f + 1)
                                t.matmul(ps_i4[0:4, 0:512], iotaPb[:, f, :], Pf[:, f % 2, 0:512],
                                         start=(f == 0), stop=(f == 31))
                                mm = t.matmul(ps_a[0][0:4, 0:512], iotaPb[:, f, :], Pf[:, f % 2, 512:1024],
                                              start=(f == 0), stop=(f == 31))
                                mm.then_inc(sT.h, 1); sT.inc()
                        assert sT.n == TP + 32
                        VP2 = VP + 1 + 32
                        TP2 = TP + 32
                        assert sV.n == VP2 and sT.n == TP2

                        @block.vector
                        def _(v, TP2=TP2, rho=rho):
                            v.wait_ge(sT.h, TP2)
                            v.tensor_copy(idxs_s[0:4, 0:512], ps_i4[0:4, 0:512])
                            v.tensor_copy(idxs_s[0:4, 512:1024], ps_a[0][0:4, 0:512]).then_inc(sV.h, 1); sV.inc()

                        @block.tensor
                        def _(t, VP2=VP2, TP2=TP2, rho=rho):
                            t.wait_ge(sV.h, VP2 + 1)
                            for rc in range(8):
                                if rc >= 1:
                                    t.wait_ge(sV.h, VP2 + 1 + rc)
                                t.transpose(ps_s[0][0:128, 0:4], idxs_s[0:4, rc * 128:(rc + 1) * 128],
                                            identF[0:4, 0:4]).then_inc(sT.h, 1); sT.inc()

                        @block.vector
                        def _(v, TP2=TP2, rho=rho):
                            for rc in range(8):
                                v.wait_ge(sT.h, TP2 + rc + 1)
                                v.tensor_copy(itp[:, 0:4], ps_s[0][0:128, 0:4])
                                v.drain()
                                v.scalar_tensor_tensor(iv_f[:, 0:1], itp[:, 0:1], 64.0, itp[:, 1:2], ALU.mult, ALU.add)
                                v.scalar_tensor_tensor(iv_f[:, 1:2], itp[:, 2:3], 64.0, itp[:, 3:4], ALU.mult, ALU.add)
                                v.drain()
                                v.tensor_copy(idxq[:, rho, rc:rc + 1], iv_f[:, 0:1])
                                v.tensor_copy(idx2q[:, rho, rc:rc + 1], iv_f[:, 1:2]).then_inc(sV.h, 1); sV.inc()
                        assert sV.n == VP2 + 1 + 8 and sT.n == TP2 + 8

                    VSc = sV.n

                    @block.vector
                    def _(v):
                        for rho in range(2):
                            cs_ = rho * 32
                            v.tensor_scalar(msk_s[:, 0, :], scores[:, cs_:cs_ + 32], -1.0, 1.0, ALU.mult, ALU.add)
                            v.drain()
                            v.tensor_add(msk_s[:, 1, :], scores[:, cs_:cs_ + 32], msk_s[:, 0, :])
                            v.drain()
                            for j in range(8):
                                v.tensor_copy(screp[:, :, rho * 8 + j], msk_s[:, 1, :])
                        v.drain().then_inc(sV.h, 1); sV.inc()

                    @block.gpsimd
                    def _(g):
                        g.wait_ge(sV.h, VSc + 1)
                        g.dma_start(out=sc_d[:, :], in_=screp[:, :, :].rearrange("p a b -> p (a b)")).then_inc(sW.h, 16); sW.inc(16)
                    sc_d_W = sW.n

                    if stage == 2:
                        @block.sync
                        def _(sync, _vn=None):
                            sync.wait_ge(sV.h, sV.n)
                            sync.wait_ge(sA.h, sA.n)
                            sync.wait_ge(sW.h, sW.n)
                            sync.dma_start(out=dbg_sel[0:128, 0:64], in_=scores[:, :]).then_inc(sO.h, 16); sO.inc(16)
                            sync.dma_start(out=dbg_s[0:128, 0:64], in_=s_sb[:, :]).then_inc(sO.h, 16); sO.inc(16)
                            sync.dma_start(out=dbg_sel[0:128, 64:128],
                                           in_=maskr[:, :, :].rearrange("p a b -> p (a b)")).then_inc(sO.h, 16); sO.inc(16)
                            sync.dma_start(out=dbg_sel[0:128, 128:192],
                                           in_=rankr[:, :, :].rearrange("p a b -> p (a b)")).then_inc(sO.h, 16); sO.inc(16)
                            sync.dma_start(out=dbg_idx[:, :, :], in_=idxq[:, :, :]).then_inc(sO.h, 16); sO.inc(16)
                            sync.dma_start(out=dbg_idx2[:, :, :], in_=idx2q[:, :, :]).then_inc(sO.h, 16); sO.inc(16)


                if stage >= 3:
                    sc16 = sb("sc16", [1, 16], F32, at)

                    def emit_bound(chunks, scratch, gate_V):
                        for (src, nf, outc) in chunks:
                            ntile = (nf + 511) // 512
                            aa2, tt2, vv2 = sA.n, sT.n, sV.n

                            @block.scalar
                            def _(s, src=src, nf=nf, vv2=vv2, gate_V=gate_V, scratch=scratch):
                                s.wait_ge(sV.h, max(vv2, gate_V))
                                s.activation(scratch[:, 0:nf], src, AF.Square)
                                s.drain().then_inc(sA.h, 1); sA.inc()

                            @block.tensor
                            def _(t, aa2=aa2, ntile=ntile, nf=nf, vv2=vv2, scratch=scratch):
                                t.wait_ge(sA.h, aa2 + 1)
                                for j in range(ntile):
                                    nn = min(512, nf - j * 512)
                                    if j >= 1:
                                        t.wait_ge(sV.h, vv2 + j)
                                    t.matmul(ps_z[0:1, 0:nn], onescolb[:, :], scratch[:, j * 512:j * 512 + nn],
                                             start=True, stop=True).then_inc(sT.h, 1); sT.inc()

                            @block.vector
                            def _(v, tt2=tt2, ntile=ntile, nf=nf, outc=outc):
                                for j in range(ntile):
                                    nn = min(512, nf - j * 512)
                                    v.wait_ge(sT.h, tt2 + j + 1)
                                    v.tensor_reduce(sc16[0:1, 15:16], ps_z[0:1, 0:nn], axis=AX.X, op=ALU.max)
                                    v.drain()
                                    v.tensor_max(sc16[0:1, outc:outc + 1], sc16[0:1, outc:outc + 1],
                                                 sc16[0:1, 15:16])
                                    v.drain().then_inc(sV.h, 1); sV.inc()

                    def emit_bound_to_negb(dst):
                        VB1, AB1 = sV.n, sA.n

                        @block.scalar
                        def _(s):
                            s.wait_ge(sV.h, VB1)
                            s.activation(sc16[0:1, 9:10], sc16[0:1, 0:1], AF.Ln)
                            s.drain()
                            s.activation(sc16[0:1, 10:11], sc16[0:1, 2:3], AF.Ln)
                            s.drain().then_inc(sA.h, 1); sA.inc()

                        @block.vector
                        def _(v):
                            v.wait_ge(sA.h, AB1 + 1)
                            v.drain()
                            v.tensor_add(sc16[0:1, 11:12], sc16[0:1, 9:10], sc16[0:1, 10:11]).then_inc(sV.h, 1); sV.inc()

                        @block.scalar
                        def _(s):
                            s.wait_ge(sV.h, VB1 + 1)
                            s.activation(sc16[0:1, 12:13], sc16[0:1, 11:12], AF.Exp, scale=0.5).then_inc(sA.h, 1); sA.inc()

                        @block.vector
                        def _(v):
                            v.wait_ge(sA.h, AB1 + 2)
                            v.drain()
                            v.tensor_scalar(sc16[0:1, 13:14], sc16[0:1, 12:13], -0.125, None, ALU.mult).then_inc(sV.h, 1); sV.inc()
                        VB2, TB2 = sV.n, sT.n

                        @block.tensor
                        def _(t):
                            t.wait_ge(sV.h, VB2)
                            t.matmul(ps_s[0][:, 0:1], onesrow[:, :], sc16[0:1, 13:14], start=True, stop=True).then_inc(sT.h, 1); sT.inc()

                        @block.vector
                        def _(v, dst=dst):
                            v.wait_ge(sT.h, TB2 + 1)
                            v.tensor_copy(dst[:, :], ps_s[0][:, 0:1]).then_inc(sV.h, 1); sV.inc()
                        return sV.n

                    with ExitStack() as al:
                        Qpl = sb("Qpl", [128, 4, 2048], BF16, al)
                        Kpl = sb("Kpl", [128, 4, 2304], BF16, al)
                        Vpl = sb("Vpl", [128, 18, 512], BF16, al)
                        negb_s = sb("negb_s", [128, 1], F32, al)
                        zra = sb("zra", [1, 2, 512], F32, al)
                        zbv = sb("zbv", [128, 512], F32, al)
                        tmp32 = sb("tmp32", [128, 2, 512], F32, al)
                        probs = sb("probs", [128, 2, 512], BF16, al)

                        with ExitStack() as ap_:
                            wsl = sb("wsl", [128, 3, 8, 512], BF16, ap_)
                            WL0 = sW.n
                            VL0, TL0 = sV.n, sT.n

                            @block.gpsimd
                            def _(g):
                                g.wait_ge(sV.h, VL0)
                                for j in range(3):
                                    g.dma_start(out=wsl[:, j, :, :],
                                                in_=qkv_w.rearrange("(m p) f -> p m f", p=128)[:, :, j * 512:(j + 1) * 512]
                                                ).then_inc(sW.h, 16); sW.inc(16)

                            proj = []
                            for hdc in range(4):
                                for qt in range(4):
                                    proj.append(("q", hdc, qt, 512))
                            for hdc in range(4):
                                for kt in range(5):
                                    proj.append(("k", hdc, kt, 512 if kt < 4 else 256))
                            for k in range(18):
                                proj.append(("v", k, 0, 512))

                            @block.tensor
                            def _(t):
                                t.wait_ge(sSc.h, xnT_S_done)
                                for i, (kind, a1, a2, nn) in enumerate(proj):
                                    if i == 0:
                                        t.wait_ge(sW.h, WL0 + 16)
                                    if kind == "k" and a1 == 0 and a2 == 0:
                                        t.wait_ge(sW.h, WL0 + 32)
                                    if kind == "v" and a1 == 0:
                                        t.wait_ge(sW.h, WL0 + 48)
                                    if i >= 2:
                                        t.wait_ge(sV.h, VL0 + i - 1)
                                    for d in range(8):
                                        if kind == "q":
                                            mm = t.matmul(ps_a[i % 2][:, 0:nn], wsl[:, 0, d, a1 * 128:(a1 + 1) * 128],
                                                          xnT[:, d, 1 + 4 * a2:1 + 4 * a2 + 4, :],
                                                          start=(d == 0), stop=(d == 7))
                                        elif kind == "k":
                                            mm = t.matmul(ps_a[i % 2][:, 0:nn], wsl[:, 1, d, a1 * 128:(a1 + 1) * 128],
                                                          xnT[:, d, 4 * a2:4 * a2 + (4 if a2 < 4 else 2), :],
                                                          start=(d == 0), stop=(d == 7))
                                        else:
                                            mm = t.matmul(ps_a[i % 2][:, 0:nn], xnT[:, d, a1, :], wsl[:, 2, d, :],
                                                          start=(d == 0), stop=(d == 7))
                                    mm.then_inc(sT.h, 1); sT.inc()

                            @block.vector
                            def _(v):
                                for i, (kind, a1, a2, nn) in enumerate(proj):
                                    v.wait_ge(sT.h, TL0 + i + 1)
                                    if kind == "q":
                                        v.tensor_copy(Qpl[:, a1, a2 * 512:(a2 + 1) * 512], ps_a[i % 2][:, :]).then_inc(sV.h, 1); sV.inc()
                                    elif kind == "k":
                                        v.tensor_copy(Kpl[:, a1, a2 * 512:a2 * 512 + nn], ps_a[i % 2][:, 0:nn]).then_inc(sV.h, 1); sV.inc()
                                    else:
                                        v.tensor_copy(Vpl[:, a1, :], ps_a[i % 2][:, :]).then_inc(sV.h, 1); sV.inc()
                            assert sT.n == TL0 + len(proj) and sV.n == VL0 + len(proj)

                        VB0 = sV.n

                        @block.vector
                        def _(v):
                            v.wait_ge(sV.h, VB0)
                            v.memset(sc16[0:1, 0:1], 0.0)
                            v.memset(sc16[0:1, 2:3], 0.0)
                            v.drain().then_inc(sV.h, 1); sV.inc()

                        with ExitStack() as ab_:
                            nrm = sb("nrm", [128, 2304], BF16, ab_)
                            emit_bound([(Qpl[:, c_, :], 2048, 0) for c_ in range(4)] +
                                       [(Kpl[:, c_, :], 2304, 2) for c_ in range(4)], nrm, VB0 + 1)
                        negb_V = emit_bound_to_negb(negb_s)

                        with ExitStack() as am:
                            mload = sb("mload", [128, 6, 512], BF16, am)
                            OHl = sb("OHl", [128, 4, 2048], BF16, am)
                            loutwb = sb("loutwb", [128, 4, 1024], BF16, am)

                            @block.gpsimd
                            def _(g):
                                g.wait_ge(sV.h, negb_V)
                                g.dma_start(out=loutwb[:, :, :],
                                            in_=lout_w.rearrange("(m p) f -> p m f", p=128)).then_inc(sW.h, 16); sW.inc(16)
                            loutw_W = sW.n

                            mask_W = {}
                            MD0 = sD.n
                            V_loop_base = sV.n
                            qt_consumed_V = {qt: V_loop_base + 60 * (qt + 1) for qt in range(-1, 4)}

                            @block.sync
                            def _(sync):
                                sync.wait_ge(sV.h, negb_V)
                                for qt in range(4):
                                    if qt >= 1:
                                        sync.wait_ge(sV.h, qt_consumed_V[qt - 1])
                                    for sl in range(6):
                                        sync.dma_start(out=mload[:, sl, :],
                                                       in_=masks_c[qt, sl, :, :]).then_inc(sD.h, 16); sD.inc(16)
                                    mask_W[qt] = sD.n

                            for qt in range(4):
                                for hdc in range(4):
                                    it_ = qt * 4 + hdc
                                    for hp in range(2):
                                        h = 2 * hdc + hp
                                        hb = hp * 64
                                        T1, V1x, A1x = sT.n, sV.n, sA.n

                                        @block.tensor
                                        def _(t, qt=qt, hdc=hdc, hb=hb, T1=T1, V1x=V1x, it_=it_, hp=hp):
                                            t.wait_ge(sV.h, V1x)
                                            if hdc == 0 and hp == 0:
                                                t.wait_ge(sD.h, mask_W[qt])
                                            if it_ == 0 and hp == 0:
                                                t.wait_ge(sW.h, loutw_W)
                                            for sl in range(6):
                                                if sl >= 2:
                                                    t.wait_ge(sV.h, V1x + sl - 1)
                                                t.matmul(ps_a[sl % 2][:, :],
                                                         Kpl[hb:hb + 64, hdc, (4 * qt + sl) * 128:(4 * qt + sl + 1) * 128],
                                                         Qpl[hb:hb + 64, hdc, qt * 512:(qt + 1) * 512],
                                                         start=True, stop=True).then_inc(sT.h, 1); sT.inc()

                                        @block.vector
                                        def _(v, qt=qt, T1=T1, A1x=A1x):
                                            for sl in range(6):
                                                v.wait_ge(sT.h, T1 + sl + 1)
                                                if sl >= 2:
                                                    v.wait_ge(sA.h, A1x + sl - 1)
                                                v.scalar_tensor_tensor(tmp32[:, sl % 2, :], ps_a[sl % 2][:, :], 0.125,
                                                                       mload[:, sl, :], ALU.mult,
                                                                       ALU.add).then_inc(sV.h, 1); sV.inc()

                                        T2x = sT.n

                                        @block.scalar
                                        def _(s, V1x=V1x, T2x=T2x, T1=T1):
                                            s.wait_ge(sT.h, T1)
                                            for sl in range(6):
                                                s.wait_ge(sV.h, V1x + sl + 1)
                                                if sl >= 2:
                                                    s.wait_ge(sT.h, T2x + 2 * (sl - 1))
                                                s.activation(probs[:, sl % 2, :], tmp32[:, sl % 2, :], AF.Exp,
                                                             bias=negb_s[:, :]).then_inc(sA.h, 1); sA.inc()

                                        @block.tensor
                                        def _(t, qt=qt, h=h, hb=hb, A1x=A1x, it_=it_, hp=hp, V1x=V1x):
                                            for sl in range(6):
                                                t.wait_ge(sA.h, A1x + sl + 1)
                                                t.matmul(ps_z[0:1, :], onescolb[:, :], probs[:, sl % 2, :],
                                                         start=(sl == 0), stop=(sl == 5)).then_inc(sT.h, 1); sT.inc()
                                                t.matmul(ps_b[it_ % 2][hb:hb + 64, :],
                                                         Vpl[:, 4 * qt + sl, h * 64:h * 64 + 64],
                                                         probs[:, sl % 2, :], start=(sl == 0), stop=(sl == 5),
                                                         tile_position=(0, hb)).then_inc(sT.h, 1); sT.inc()
                                        T3x = sT.n

                                        @block.vector
                                        def _(v, hp=hp, T3x=T3x):
                                            v.wait_ge(sT.h, T3x)
                                            v.tensor_copy(zra[0:1, hp, :], ps_z[0:1, :])
                                            v.drain()
                                            v.tensor_scalar(zra[0:1, hp, :], zra[0:1, hp, :], 1e-38, None, ALU.max)
                                            v.drain()
                                            v.reciprocal(zra[0:1, hp, :], zra[0:1, hp, :])
                                            v.drain().then_inc(sV.h, 1); sV.inc()

                                    TE, VE = sT.n, sV.n

                                    @block.tensor
                                    def _(t, VE=VE):
                                        t.wait_ge(sV.h, VE)
                                        t.matmul(ps_s[0][0:64, 0:512], onesrow[0:1, 0:64], zra[0:1, 0, :],
                                                 start=True, stop=True).then_inc(sT.h, 1); sT.inc()
                                        t.matmul(ps_s[0][64:128, 0:512], onesrow[0:1, 0:64], zra[0:1, 1, :],
                                                 start=True, stop=True, tile_position=(0, 64)).then_inc(sT.h, 1); sT.inc()

                                    @block.vector
                                    def _(v, TE=TE, qt=qt, hdc=hdc, it_=it_):
                                        v.wait_ge(sT.h, TE + 2)
                                        v.tensor_copy(zbv[:, :], ps_s[0][:, 0:512])
                                        v.drain()
                                        v.tensor_tensor(OHl[:, hdc, qt * 512:(qt + 1) * 512], ps_b[it_ % 2][:, :],
                                                        zbv[:, :], ALU.mult).then_inc(sV.h, 1); sV.inc()
                                pass
                            for qt in range(4):
                                assert qt_consumed_V[qt] <= sV.n

                            if stage == 3:
                                @block.sync
                                def _(sync):
                                    sync.wait_ge(sV.h, sV.n)
                                    sync.dma_start(out=dbg_lq[:, :, :], in_=Qpl[:, :, :]).then_inc(sO.h, 16); sO.inc(16)
                                    sync.dma_start(out=dbg_lk[:, :, :], in_=Kpl[:, :, :]).then_inc(sO.h, 16); sO.inc(16)
                                    sync.dma_start(out=dbg_loh[:, :, :], in_=OHl[:, :, :]).then_inc(sO.h, 16); sO.inc(16)
                                    sync.dma_start(out=dbg_lw[:, :, :], in_=loutwb[:, :, :]).then_inc(sO.h, 16); sO.inc(16)

                            TO, VO = sT.n, sV.n

                            @block.tensor
                            def _(t):
                                t.wait_ge(sV.h, VO)
                                for i2 in range(32):
                                    tc, dmh = i2 // 2, i2 % 2
                                    if i2 >= 2:
                                        t.wait_ge(sV.h, VO + i2 - 1)
                                    for hdc in range(4):
                                        mm = t.matmul(ps_a[i2 % 2][:, :], OHl[:, hdc, tc * 128:(tc + 1) * 128],
                                                      loutwb[:, hdc, dmh * 512:(dmh + 1) * 512],
                                                      start=(hdc == 0), stop=(hdc == 3))
                                    mm.then_inc(sT.h, 1); sT.inc()

                            @block.vector
                            def _(v):
                                for i2 in range(32):
                                    tc, dmh = i2 // 2, i2 % 2
                                    v.wait_ge(sT.h, TO + i2 + 1)
                                    v.tensor_tensor(RES[:, tc, dmh * 512:(dmh + 1) * 512], ps_a[i2 % 2][:, :],
                                                    RES[:, tc, dmh * 512:(dmh + 1) * 512], ALU.add).then_inc(sV.h, 1); sV.inc()
                            assert sV.n == VO + 32

                if stage >= 4:
                    with ExitStack() as ah:
                        Qh = sb("Qh", [128, 4, 1024], BF16, ah)
                        Kh = sb("Kh", [128, 4, 1024], BF16, ah)
                        Vh = sb("Vh", [128, 8, 512], BF16, ah)
                        ropeb = sb("ropeb", [128, 4, 1024], BF16, ah)
                        scg = sb("scg", [128, 2, 8, 16], F32, ah)
                        nrmh = sb("nrmh", [128, 1024], BF16, ah)
                        negbh = sb("negbh", [128, 1], F32, ah)
                        sch16 = sb("sch16", [1, 16], F32, ah)
                        zrah = sb("zrah", [1, 2, 512], F32, ah)
                        zbvh = sb("zbvh", [128, 512], F32, ah)
                        probh = sb("probh", [128, 8, 512], BF16, ah)
                        probn = sb("probn", [1, 2, 512], BF16, ah)
                        nkb = sb("nkb", [128, 16], BF16, ah)
                        nvb = sb("nvb", [1, 8, 64], BF16, ah)
                        sq64 = sb("sq64", [64, 16], BF16, ah)
                        OHh = sb("OHh", [128, 4, 1024], BF16, ah)
                        sel0b = sb("sel0b", [128, 16], F32, ah)
                        sel1b = sb("sel1b", [128, 16], F32, ah)
                        srcu = sb("srcu", [128, 16], U32, ah)
                        src32 = sb("src32", [128, 32], F32, ah)
                        src32b = sb("src32b", [128, 32], F32, ah)

                        with ExitStack() as ahp:
                            nqT = xnT[:, :, 0:8, :]
                            nkT = xnT[:, :, 9:17, :]
                            rtb = sb("rtb", [128, 2, 1024], F32, ahp)
                            nrt = sb("nrt", [128, 2, 1024], BF16, ahp)
                            hst = sb("hst", [128, 8], F32, ahp)
                            hws = sb("hws", [128, 2, 8, 512], BF16, ahp)
                            rpg = sb("rpg", [128, 2, 64], F32, ahp)

                            WH0, VH0, AH0, SH0 = sW.n, sV.n, sA.n, sSc.n

                            @block.sync
                            def _(sync):
                                sync.wait_ge(sV.h, VH0)
                                sync.dma_start(out=sel0b[:, :], in_=sel0_c[:, :]).then_inc(sD.h, 16); sD.inc(16)
                                sync.dma_start(out=sel1b[:, :], in_=sel1_c[:, :]).then_inc(sD.h, 16); sD.inc(16)

                            @block.gpsimd
                            def _(g):
                                g.wait_ge(sV.h, VH0)
                                g.dma_start(out=nkb[0:64, :], in_=hnull_c[:, :]).then_inc(sW.h, 16); sW.inc(16)
                                g.dma_start(out=nkb[64:128, :], in_=hnull_c[:, :]).then_inc(sW.h, 16); sW.inc(16)
                                g.dma_start(out=nvb[:, :, :], in_=hnull_vrow[:, :, :]).then_inc(sW.h, 16); sW.inc(16)
                                for j in range(16):
                                    rho, rc = j // 8, j % 8
                                    if j >= 2:
                                        g.wait_ge(sV.h, VH0 + 2 * (j - 1))
                                    g.indirect_dma_start(
                                        out=rtb[:, j % 2, :], out_offset=None,
                                        in_=z_pair.ap(),
                                        in_offset=IndirectOffsetOnAxis(idxq[:, rho, rc:rc + 1], axis=0)
                                        ).then_inc(sW.h, 16); sW.inc(16)
                                    g.indirect_dma_start(
                                        out=scg[:, rho, rc, :], out_offset=None,
                                        in_=sc_d.ap(),
                                        in_offset=IndirectOffsetOnAxis(idx2q[:, rho, rc:rc + 1], axis=0)
                                        ).then_inc(sW.h, 16); sW.inc(16)

                            @block.scalar
                            def _(s):
                                for j in range(16):
                                    rho, rc = j // 8, j % 8
                                    r = j % 2
                                    s.wait_ge(sW.h, WH0 + 16 * 2 + 32 * j + 16)
                                    s.activation(nrmh[:, :], rtb[:, r, :], AF.Square, accum_out=hst[:, 0:1])
                                    s.drain().then_inc(sA.h, 1); sA.inc()
                                    s.wait_ge(sV.h, VH0 + 2 * j + 1)
                                    s.activation(hst[:, 2:3], hst[:, 1:2], AF.Ln)
                                    s.drain()
                                    s.activation(hst[:, 3:4], hst[:, 2:3], AF.Exp, scale=-0.5)
                                    s.drain().then_inc(sA.h, 1); sA.inc()
                                    s.wait_ge(sV.h, VH0 + 2 * j + 2)
                                    s.dma_start_transpose((nqT if rho == 0 else nkT)[:, :, rc, :],
                                                          nrt[:, r, :]).then_inc(sSc.h, 16); sSc.inc(16)

                            @block.vector
                            def _(v):
                                for j in range(16):
                                    r = j % 2
                                    v.wait_ge(sA.h, AH0 + 2 * j + 1)
                                    if j >= 2:
                                        v.wait_ge(sSc.h, SH0 + 16 * (j - 1))
                                    v.drain()
                                    v.tensor_scalar(hst[:, 1:2], hst[:, 0:1], 1.0, 1e-24, ALU.max, ALU.max).then_inc(sV.h, 1); sV.inc()
                                    v.wait_ge(sA.h, AH0 + 2 * j + 2)
                                    v.drain()
                                    v.tensor_scalar(hst[:, 4:5], hst[:, 3:4], 32.0, None, ALU.mult)
                                    v.drain()
                                    v.tensor_scalar(nrt[:, r, :], rtb[:, r, :], hst[:, 4:5], None, ALU.mult)
                                    v.tensor_tensor(nrt[:, r, :], nrt[:, r, :], hgamb[:, :], ALU.mult).then_inc(sV.h, 1); sV.inc()
                            assert sV.n == VH0 + 32 and sA.n == AH0 + 32 and sSc.n == SH0 + 256
                            nT_S_done = sSc.n

                            WR0, TR0, VR1 = sW.n, sT.n, sV.n

                            @block.gpsimd
                            def _(g):
                                for j in range(32):
                                    tab, rc = j // 8, j % 8
                                    rho = 0 if tab < 2 else 1
                                    srct = cos_t if tab % 2 == 0 else sin_t
                                    if j >= 2:
                                        g.wait_ge(sT.h, TR0 + j - 1)
                                    g.indirect_dma_start(
                                        out=rpg[:, j % 2, :], out_offset=None,
                                        in_=srct,
                                        in_offset=IndirectOffsetOnAxis(idxq[:, rho, rc:rc + 1], axis=0)
                                        ).then_inc(sW.h, 16); sW.inc(16)

                            @block.tensor
                            def _(t):
                                for j in range(32):
                                    t.wait_ge(sW.h, WR0 + 16 * (j + 1))
                                    if j >= 1:
                                        t.wait_ge(sV.h, VR1 + j)
                                    t.transpose(ps_s[0][0:64, 0:128], rpg[:, j % 2, :], identF[:, :]).then_inc(sT.h, 1); sT.inc()

                            @block.vector
                            def _(v):
                                for j in range(32):
                                    tab, rc = j // 8, j % 8
                                    v.wait_ge(sT.h, TR0 + j + 1)
                                    v.tensor_copy(ropeb[0:64, tab, rc * 128:(rc + 1) * 128], ps_s[0][0:64, 0:128])
                                    v.tensor_copy(ropeb[64:128, tab, rc * 128:(rc + 1) * 128], ps_s[0][0:64, 0:128]).then_inc(sV.h, 1); sV.inc()
                            assert sV.n == VR1 + 32 and sT.n == TR0 + 32
                            rope_V = sV.n

                            def rotbuild(gateW, gateT):
                                VRB = sV.n

                                @block.vector
                                def _(v, gateW=gateW, gateT=gateT):
                                    v.wait_ge(sW.h, gateW)
                                    if gateT is not None:
                                        v.wait_ge(sT.h, gateT)
                                    for d in range(8):
                                        v.tensor_scalar(hws[:, 1, d, :].rearrange("p (h e) -> p h e", e=64)[:, :, 0:32],
                                                        hws[:, 0, d, :].rearrange("p (h e) -> p h e", e=64)[:, :, 32:64],
                                                        -1.0, None, ALU.mult)
                                        v.tensor_copy(hws[:, 1, d, :].rearrange("p (h e) -> p h e", e=64)[:, :, 32:64],
                                                      hws[:, 0, d, :].rearrange("p (h e) -> p h e", e=64)[:, :, 0:32])
                                    v.drain().then_inc(sV.h, 1); sV.inc()
                                return sV.n

                            def projgroup_rope(srcT, dst, cosc, sinc, rot_V):
                                T0_, V0_ = sT.n, sV.n
                                steps = [(hdc, rt, sl_) for hdc in range(4) for rt in range(2) for sl_ in range(2)]

                                @block.tensor
                                def _(t, srcT=srcT, rot_V=rot_V, T0_=T0_, V0_=V0_, steps=steps):
                                    t.wait_ge(sV.h, rot_V)
                                    t.wait_ge(sSc.h, nT_S_done)
                                    for i, (hdc, rt, sl_) in enumerate(steps):
                                        if i >= 2:
                                            t.wait_ge(sV.h, V0_ + i - 1)
                                        for d in range(8):
                                            mm = t.matmul(ps_a[i % 2][:, :], hws[:, sl_, d, hdc * 128:(hdc + 1) * 128],
                                                          srcT[:, d, 4 * rt:4 * rt + 4, :], start=(d == 0), stop=(d == 7))
                                        mm.then_inc(sT.h, 1); sT.inc()

                                @block.vector
                                def _(v, dst=dst, cosc=cosc, sinc=sinc, T0_=T0_, steps=steps):
                                    for i, (hdc, rt, sl_) in enumerate(steps):
                                        v.wait_ge(sT.h, T0_ + i + 1)
                                        if sl_ == 0:
                                            v.tensor_tensor(dst[:, hdc, rt * 512:(rt + 1) * 512], ps_a[i % 2][:, :],
                                                            ropeb[:, cosc, rt * 512:(rt + 1) * 512],
                                                            ALU.mult).then_inc(sV.h, 1); sV.inc()
                                        else:
                                            v.tensor_tensor(zbvh[:, :], ps_a[i % 2][:, :],
                                                            ropeb[:, sinc, rt * 512:(rt + 1) * 512], ALU.mult)
                                            v.drain()
                                            v.tensor_tensor(dst[:, hdc, rt * 512:(rt + 1) * 512],
                                                            dst[:, hdc, rt * 512:(rt + 1) * 512],
                                                            zbvh[:, :], ALU.add).then_inc(sV.h, 1); sV.inc()
                                return sT.n, sV.n

                            WQ0 = sW.n

                            @block.gpsimd
                            def _(g):
                                g.wait_ge(sV.h, rope_V)
                                g.dma_start(out=hws[:, 0, :, :],
                                            in_=hq_w.rearrange("(m p) f -> p m f", p=128)).then_inc(sW.h, 16); sW.inc(16)
                            q_rot_V = rotbuild(sW.n, None)
                            qT_done, qV_done = projgroup_rope(nqT, Qh, 0, 1, q_rot_V)

                            WK0 = sW.n

                            @block.gpsimd
                            def _(g):
                                g.wait_ge(sT.h, qT_done)
                                g.dma_start(out=hws[:, 0, :, :],
                                            in_=hkv_wk.rearrange("(m p) f -> p m f", p=128)).then_inc(sW.h, 16); sW.inc(16)
                            k_rot_V = rotbuild(sW.n, qT_done)
                            kT_done, kV_done = projgroup_rope(nkT, Kh, 2, 3, k_rot_V)

                            WV0 = sW.n

                            @block.gpsimd
                            def _(g):
                                g.wait_ge(sT.h, kT_done)
                                g.dma_start(out=hws[:, 0, :, :],
                                            in_=hkv_wv.rearrange("(m p) f -> p m f", p=128)).then_inc(sW.h, 16); sW.inc(16)
                            WV1 = sW.n
                            TV0, VV0 = sT.n, sV.n

                            @block.tensor
                            def _(t):
                                t.wait_ge(sW.h, WV1)
                                for rc in range(8):
                                    if rc >= 2:
                                        t.wait_ge(sV.h, VV0 + rc - 1)
                                    for d in range(8):
                                        mm = t.matmul(ps_a[rc % 2][:, :], nkT[:, d, rc, :], hws[:, 0, d, :],
                                                      start=(d == 0), stop=(d == 7))
                                    mm.then_inc(sT.h, 1); sT.inc()

                            @block.vector
                            def _(v):
                                for rc in range(8):
                                    v.wait_ge(sT.h, TV0 + rc + 1)
                                    v.tensor_scalar(Vh[:, rc, :], ps_a[rc % 2][:, :], scg[:, 1, rc, 8:9],
                                                    None, ALU.mult).then_inc(sV.h, 1); sV.inc()
                            assert sV.n == VV0 + 8

                        VRA = sV.n
                        rope_apply_V = VRA
                        VBH = sV.n

                        @block.vector
                        def _(v):
                            v.wait_ge(sV.h, VBH)
                            v.memset(sc16[0:1, 0:1], 0.0)
                            v.memset(sc16[0:1, 2:3], 0.0)
                            v.drain().then_inc(sV.h, 1); sV.inc()

                        emit_bound([(Qh[:, c_, :], 1024, 0) for c_ in range(4)] +
                                   [(Kh[:, c_, :], 1024, 2) for c_ in range(4)], nrmh, VBH + 1)

                        ABn, TBn, VBn = sA.n, sT.n, sV.n

                        @block.scalar
                        def _(s):
                            s.wait_ge(sV.h, VBn)
                            s.activation(sq64[:, :], nkb[0:64, :], AF.Square)
                            s.drain().then_inc(sA.h, 1); sA.inc()

                        @block.tensor
                        def _(t):
                            t.wait_ge(sA.h, ABn + 1)
                            t.matmul(ps_z[0:1, 0:16], onescolb[0:64, :], sq64[:, :], start=True, stop=True).then_inc(sT.h, 1); sT.inc()

                        @block.vector
                        def _(v):
                            v.wait_ge(sT.h, TBn + 1)
                            v.tensor_reduce(sc16[0:1, 15:16], ps_z[0:1, 0:8], axis=AX.X, op=ALU.max)
                            v.drain()
                            v.tensor_max(sc16[0:1, 2:3], sc16[0:1, 2:3], sc16[0:1, 15:16])
                            v.drain().then_inc(sV.h, 1); sV.inc()

                        negbh_V = emit_bound_to_negb(negbh)

                        for qt2 in range(2):
                            for hdc in range(4):
                                it_ = qt2 * 4 + hdc
                                for hp in range(2):
                                    h = 2 * hdc + hp
                                    hb = hp * 64
                                    T1h, V1h, A1h = sT.n, sV.n, sA.n

                                    @block.tensor
                                    def _(t, qt2=qt2, hdc=hdc, hb=hb, h=h, T1h=T1h, V1h=V1h, A1h=A1h, it_=it_, hp=hp):
                                        t.wait_ge(sV.h, V1h)
                                        for rc in range(9):
                                            if rc >= 2:
                                                t.wait_ge(sA.h, A1h + rc - 1)
                                            if rc < 8:
                                                t.matmul(ps_a[rc % 2][:, :],
                                                         Kh[hb:hb + 64, hdc, rc * 128:(rc + 1) * 128],
                                                         Qh[hb:hb + 64, hdc, qt2 * 512:(qt2 + 1) * 512],
                                                         start=True, stop=True).then_inc(sT.h, 1); sT.inc()
                                            else:
                                                t.matmul(ps_i4[0:1, 0:512], nkb[hb:hb + 64, h:h + 1],
                                                         Qh[hb:hb + 64, hdc, qt2 * 512:(qt2 + 1) * 512],
                                                         start=True, stop=True).then_inc(sT.h, 1); sT.inc()

                                    T2h = T1h + 9

                                    @block.scalar
                                    def _(s, T1h=T1h, A1h=A1h, hp=hp, qt2=qt2, T2h=T2h):
                                        s.wait_ge(sT.h, T1h)
                                        for rc in range(9):
                                            s.wait_ge(sT.h, T1h + rc + 1)
                                            if rc < 8:
                                                s.activation(probh[:, rc, :], ps_a[rc % 2][:, :], AF.Exp,
                                                             scale=0.125, bias=negbh[:, :]).then_inc(sA.h, 1); sA.inc()
                                            else:
                                                s.activation(probn[0:1, hp, :], ps_i4[0:1, 0:512], AF.Exp,
                                                             scale=0.125, bias=negbh[0:1, :]).then_inc(sA.h, 1); sA.inc()

                                    assert sT.n == T2h

                                    @block.tensor
                                    def _(t, A1h=A1h, h=h, hb=hb, it_=it_, hp=hp):
                                        for rc in range(9):
                                            t.wait_ge(sA.h, A1h + rc + 1)
                                            if rc < 8:
                                                t.matmul(ps_z[0:1, :], onescolb[:, :], probh[:, rc, :],
                                                         start=(rc == 0), stop=False).then_inc(sT.h, 1); sT.inc()
                                                t.matmul(ps_b[it_ % 2][hb:hb + 64, :], Vh[:, rc, h * 64:h * 64 + 64],
                                                         probh[:, rc, :], start=(rc == 0), stop=False,
                                                         tile_position=(0, hb)).then_inc(sT.h, 1); sT.inc()
                                            else:
                                                t.matmul(ps_z[0:1, :], onesrowb[0:1, 0:1], probn[0:1, hp, :],
                                                         start=False, stop=True).then_inc(sT.h, 1); sT.inc()
                                                t.matmul(ps_b[it_ % 2][hb:hb + 64, :], nvb[0:1, h, :],
                                                         probn[0:1, hp, :], start=False, stop=True,
                                                         tile_position=(0, hb)).then_inc(sT.h, 1); sT.inc()
                                    T3h = sT.n

                                    @block.vector
                                    def _(v, hp=hp, T3h=T3h):
                                        v.wait_ge(sT.h, T3h)
                                        v.tensor_copy(zrah[0:1, hp, :], ps_z[0:1, :])
                                        v.drain()
                                        v.tensor_scalar(zrah[0:1, hp, :], zrah[0:1, hp, :], 1e-38, None, ALU.max)
                                        v.drain()
                                        v.reciprocal(zrah[0:1, hp, :], zrah[0:1, hp, :])
                                        v.drain().then_inc(sV.h, 1); sV.inc()

                                TEh, VEh = sT.n, sV.n

                                @block.tensor
                                def _(t, VEh=VEh):
                                    t.wait_ge(sV.h, VEh)
                                    t.matmul(ps_s[0][0:64, 0:512], onesrow[0:1, 0:64], zrah[0:1, 0, :],
                                             start=True, stop=True).then_inc(sT.h, 1); sT.inc()
                                    t.matmul(ps_s[0][64:128, 0:512], onesrow[0:1, 0:64], zrah[0:1, 1, :],
                                             start=True, stop=True, tile_position=(0, 64)).then_inc(sT.h, 1); sT.inc()

                                @block.vector
                                def _(v, TEh=TEh, qt2=qt2, hdc=hdc, it_=it_):
                                    v.wait_ge(sT.h, TEh + 2)
                                    v.tensor_copy(zbvh[:, :], ps_s[0][:, 0:512])
                                    v.drain()
                                    v.tensor_tensor(OHh[:, hdc, qt2 * 512:(qt2 + 1) * 512], ps_b[it_ % 2][:, :],
                                                    zbvh[:, :], ALU.mult).then_inc(sV.h, 1); sV.inc()

                        with ExitStack() as aho:
                            hwo = sb("hwo", [128, 4, 1024], BF16, aho)
                            hvout = sb("hvout", [128, 2, 1024], F32, aho)
                            nqrow = sb("nqrow", [1, 1024], F32, aho)
                            WO0 = sW.n

                            @block.gpsimd
                            def _(g):
                                g.wait_ge(sV.h, sV.n)
                                g.dma_start(out=hwo[:, :, :],
                                            in_=hout_w.rearrange("(m p) f -> p m f", p=128)).then_inc(sW.h, 16); sW.inc(16)

                            HD0 = sD.n

                            @block.sync
                            def _(sync):
                                sync.wait_ge(sV.h, sV.n)
                                sync.dma_start(out=nqrow[:, :], in_=nullq[:, :]).then_inc(sD.h, 16); sD.inc(16)

                            THo, VHo, DHo = sT.n, sV.n, sD.n

                            @block.tensor
                            def _(t):
                                t.wait_ge(sV.h, VHo)
                                t.wait_ge(sW.h, WO0 + 16)
                                for i3 in range(16):
                                    rc, dmh = i3 // 2, i3 % 2
                                    if i3 >= 2:
                                        t.wait_ge(sV.h, VHo + i3 - 1)
                                    for hdc in range(4):
                                        mm = t.matmul(ps_a[i3 % 2][:, :], OHh[:, hdc, rc * 128:(rc + 1) * 128],
                                                      hwo[:, hdc, dmh * 512:(dmh + 1) * 512],
                                                      start=(hdc == 0), stop=(hdc == 3))
                                    mm.then_inc(sT.h, 1); sT.inc()

                            @block.vector
                            def _(v):
                                for i3 in range(16):
                                    rc, dmh = i3 // 2, i3 % 2
                                    v.wait_ge(sT.h, THo + i3 + 1)
                                    if i3 >= 4:
                                        v.wait_ge(sD.h, DHo + 16 + 16 * (i3 // 2 - 1))
                                    v.tensor_scalar(hvout[:, rc % 2, dmh * 512:(dmh + 1) * 512], ps_a[i3 % 2][:, :],
                                                    scg[:, 0, rc, 0:1], None, ALU.mult).then_inc(sV.h, 1); sV.inc()

                            @block.sync
                            def _(sync):
                                for rc in range(8):
                                    sync.wait_ge(sV.h, VHo + 2 * rc + 2)
                                    sync.dma_start(out=heavy_d.ap()[rc * 128:(rc + 1) * 128, :],
                                                   in_=hvout[:, rc % 2, :]).then_inc(sD.h, 16); sD.inc(16)
                                sync.dma_start(out=heavy_d.ap()[1024:1025, :], in_=nqrow[:, :]).then_inc(sD.h, 16); sD.inc(16)
                            heavy_store_D = sD.n

                            VG0 = sV.n

                            @block.vector
                            def _(v):
                                v.wait_ge(sV.h, VG0)
                                v.drain()
                                v.tensor_scalar(src32[:, :], rankr[:, 0, :], 1.0, -1.0, ALU.mult, ALU.add)
                                v.drain()
                                v.tensor_tensor(src32[:, :], src32[:, :], maskr[:, 0, :], ALU.mult)
                                v.drain()
                                v.tensor_scalar(src32b[:, :], maskr[:, 0, :], -1024.0, 1024.0, ALU.mult, ALU.add)
                                v.drain()
                                v.tensor_add(src32[:, :], src32[:, :], src32b[:, :])
                                v.drain()
                                v.tensor_tensor(src32b[:, 0:16], src32[:, 0:16], sel0b[:, :], ALU.mult)
                                v.tensor_tensor(src32b[:, 16:32], src32[:, 16:32], sel1b[:, :], ALU.mult)
                                v.drain()
                                v.tensor_add(src32b[:, 0:16], src32b[:, 0:16], src32b[:, 16:32])
                                v.drain()
                                v.tensor_copy(srcu[:, :], src32b[:, 0:16])
                                v.drain().then_inc(sV.h, 1); sV.inc()
                            src_V = sV.n

                            WG0 = sW.n

                            @block.gpsimd
                            def _(g):
                                g.wait_ge(sV.h, src_V)
                                g.wait_ge(sD.h, heavy_store_D)
                                for c in range(C16):
                                    if c >= 2:
                                        g.wait_ge(sV.h, src_V + (c - 1))
                                    g.indirect_dma_start(
                                        out=hvout[:, c % 2, :], out_offset=None,
                                        in_=heavy_d.ap(),
                                        in_offset=IndirectOffsetOnAxis(srcu[:, c:c + 1], axis=0)
                                        ).then_inc(sW.h, 16); sW.inc(16)

                            @block.vector
                            def _(v):
                                for c in range(C16):
                                    v.wait_ge(sW.h, WG0 + 16 * (c + 1))
                                    v.tensor_tensor(RES[:, c, :], RES[:, c, :], hvout[:, c % 2, :],
                                                    ALU.add).then_inc(sV.h, 1); sV.inc()
                            assert sV.n == src_V + C16

            return sT.n, sV.n

        done_T, done_V = emit_ffn("f1", first=True, prior_done_T=0)

        if stage >= 2:
            done_T, done_V = emit_attention(stage)
            if stage >= 5:
                done_T, done_V = emit_ffn("f2", first=False, prior_done_T=done_T,
                                          prior_done_V=done_V)

        V_end = sV.n

        @block.sync
        def _(sync):
            sync.wait_ge(sV.h, V_end)
            for c in range(C16):
                sync.dma_start(out=out[c * 128:(c + 1) * 128, :], in_=RES[:, c, :]).then_inc(sO.h, 16); sO.inc(16)
            sync.dma_start(out=dbg_cw[:, :], in_=cw_d[:, :]).then_inc(sO.h, 16); sO.inc(16)
            sync.wait_ge(sO.h, sO.n)

    return nc


def _host_consts():
    onesrow = np.ones((1, 128), np.float32)
    onescol = np.ones((128, 1), np.float32)
    iota8 = np.broadcast_to(np.arange(8, dtype=np.float32), (128, 8)).copy()
    iotar = np.broadcast_to(np.arange(1024, dtype=np.float32), (128, 1024)).copy()
    p = np.arange(128)[:, None]
    f = np.arange(32)[None, :]
    gtok = (f * 128 + p).astype(np.int64)
    rloc = (p * 32 + f).astype(np.int64)
    iotaP = np.stack([gtok >> 6, gtok & 63, rloc >> 6, rloc & 63], axis=2).astype(np.float32)
    triI = (np.arange(128)[:, None] <= np.arange(128)[None, :]).astype(np.float32)
    headE = np.zeros((8, 4, 128), np.float32)
    for hh in range(8):
        headE[hh, hh // 2, (hh % 2) * 64:(hh % 2) * 64 + 64] = 1.0
    pos = np.arange(4096, dtype=np.float64)[:, None] * (
        1.0 / 10000.0 ** (np.arange(0, 64, 2, dtype=np.float64) / 64.0))[None, :]
    pos = np.concatenate([pos, pos], axis=1)
    return dict(onesrow=onesrow, onescol=onescol, iota8=iota8, iotar=iotar,
                iotaP=iotaP, triI=triI, headE=headE,
                cos_t=np.cos(pos).astype(np.float32), sin_t=np.sin(pos).astype(np.float32))


def _host_masks(half):
    m = np.zeros((4, 6, 128, 512), np.float32)
    j = np.arange(128)[:, None]
    i = np.arange(512)[None, :]
    for qt in range(4):
        for sl in range(6):
            s_ = 4 * qt + sl
            delta = 128 * (s_ - 1) + j - (512 * qt + i)
            gk = half * 2048 + 128 * (s_ - 1) + j + 0 * i
            valid = (np.abs(delta) <= 64) & (gk >= 0) & (gk < 4096)
            m[qt, sl] = np.where(valid, 0.0, NEG)
    return m


def _prep_inputs(inputs):
    import ml_dtypes
    cst = _host_consts()
    w1 = np.asarray(inputs["L0_ffn_w1"], np.float32)
    w2 = np.asarray(inputs["L0_ffn_w2"], np.float32)
    w3 = np.asarray(inputs["L0_ffn_w3"], np.float32)
    sw1 = np.asarray(inputs["L0_ffn_sw1"], np.float32)
    sw2 = np.asarray(inputs["L0_ffn_sw2"], np.float32)
    sw3 = np.asarray(inputs["L0_ffn_sw3"], np.float32)
    w1t = np.ascontiguousarray(np.concatenate([w1, sw1[None]], 0).transpose(0, 2, 1))
    w2t = np.ascontiguousarray(np.concatenate([w2, sw2[None]], 0).transpose(0, 2, 1))
    w3t = np.ascontiguousarray(np.concatenate([w3, sw3[None]], 0).transpose(0, 2, 1))
    gwt = np.ascontiguousarray(np.asarray(inputs["L0_ffn_gate_w"], np.float32).T)

    kv_w = np.asarray(inputs["L0_attn_heavy_kv_w"], np.float32)
    kvt = kv_w.transpose(2, 0, 1)
    hkv_wk = np.ascontiguousarray(kvt[:, :, 0:64].reshape(D, 512))
    hkv_wv = np.ascontiguousarray(kvt[:, :, 64:128].reshape(D, 512))
    null_kv = np.asarray(inputs["L0_attn_heavy_null_kv"], np.float32)
    hnull_c = np.ascontiguousarray(np.concatenate([null_kv[0].T, null_kv[1].T], axis=1))
    hnull_vrow = np.ascontiguousarray(null_kv[1][None])

    rtoks = np.concatenate([np.asarray(inputs["L0_attn_q_route_tok"], np.float32),
                            np.asarray(inputs["L0_attn_kv_route_tok"], np.float32)])[None, :]

    magic = int(np.random.randint(1, 2 ** 31 - 1))
    base = dict(
        w1t=w1t, w2t=w2t, w3t=w3t, gwt=gwt,
        ln_g=np.asarray(inputs["L0_attn_ln_gamma"], np.float32)[None, :],
        ln_b=np.asarray(inputs["L0_attn_ln_beta"], np.float32)[None, :],
        qkv_w=np.asarray(inputs["L0_attn_light_qkv_w"], np.float32),
        lout_w=np.asarray(inputs["L0_attn_light_out_w"], np.float32),
        rtoks=rtoks,
        hgam=np.asarray(inputs["L0_attn_heavy_gamma"], np.float32)[None, :],
        hq_w=np.asarray(inputs["L0_attn_heavy_q_w"], np.float32),
        hkv_wk=hkv_wk, hkv_wv=hkv_wv, hnull_c=hnull_c, hnull_vrow=hnull_vrow,
        hout_w=np.asarray(inputs["L0_attn_heavy_out_w"], np.float32),
        nullq=np.asarray(inputs["L0_attn_null_q_token"], np.float32)[None, :],
        cos_t=cst["cos_t"], sin_t=cst["sin_t"],
        onesrow=cst["onesrow"], onescol=cst["onescol"], iota8=cst["iota8"],
        iotar=cst["iotar"], iotaP=cst["iotaP"].astype(ml_dtypes.bfloat16),
        triI=cst["triI"], headE=cst["headE"].astype(ml_dtypes.bfloat16),
        magicrow=np.full((1, 16), magic, np.int32),
    )

    x = np.asarray(inputs["L0_attn_x"], np.float32)
    in_maps = []
    for c in range(NCORES):
        b, h = c // 2, c % 2
        p = np.arange(128, dtype=np.uint32)[:, None]
        cc = np.arange(16, dtype=np.uint32)[None, :]
        m = dict(base)
        m["x"] = np.ascontiguousarray(x[b, h * T:(h + 1) * T, :])
        m["masks"] = _host_masks(h).astype(ml_dtypes.bfloat16)
        m["sel0"] = np.full((128, 16), 1.0 - h, np.float32)
        m["sel1"] = np.full((128, 16), float(h), np.float32)
        m["zw_idx"] = (h * 2048 + cc * 128 + p).astype(np.uint32)
        hl = 1920 if h == 1 else 0
        hr = 2048 if h == 0 else 0
        m["halo_idx"] = np.stack([hl + np.arange(128), hr + np.arange(128)], axis=1).astype(np.uint32)
        m["fwr_idx"] = np.array([c, c + 8], np.uint32)
        m["poll_idx"] = np.array([[c ^ 1], [c ^ 1]], np.uint32)
        m["srow_idx"] = np.array([2 * h, 2 * h + 1], np.uint32)
        in_maps.append(m)
    return in_maps


def kernel(**inputs):
    in_maps = _prep_inputs(inputs)
    nc = build_nc()
    res = run_bass_kernel_spmd(nc, in_maps, core_ids=list(range(NCORES)))
    y = np.empty((4, 4096, D), np.float32)
    for c in range(NCORES):
        b, h = c // 2, c % 2
        y[b, h * T:(h + 1) * T, :] = res.results[c]["out"]
    return y
